# version 01 (final)
# baseline (speedup 1.0000x reference)
import numpy as np
from contextlib import ExitStack

import concourse.bass as bass
import concourse.mybir as mybir
from concourse.bass import IndirectOffsetOnAxis
from concourse.bass_utils import run_bass_kernel_spmd

F32 = mybir.dt.float32
BF16 = mybir.dt.bfloat16
I32 = mybir.dt.int32
AF = mybir.ActivationFunctionType
ALU = mybir.AluOpType

PI = float(np.pi)
MAGIC = 1.5 * 2.0 ** 23
LN16 = float(np.log(16.0))

ENGS = ("sync", "gpsimd", "vector", "scalar", "tensor")
DBG = False
import os
S7_SKIP_INDIRECT = os.environ.get("S7_SKIP_INDIRECT", "0") == "1"
S7_SKIP_TOPK = os.environ.get("S7_SKIP_TOPK", "0") == "1"
S7_SKIP_UP = os.environ.get("S7_SKIP_UP", "0") == "1"
S7_NT = int(os.environ.get("S7_NT", "16"))
S7_PH = int(os.environ.get("S7_PH", "99"))

B_PER_CORE = 2
T = 4096
D = 1024
NC_CHUNK = D // 128
TT = 512
NTILE = T // TT
NCHUNK = T // 128



class Prog:

    def __init__(self, nc):
        self.nc = nc
        self.q = {e: [] for e in ENGS}
        self.sems = {}

    def sem(self, name):
        if name not in self.sems:
            self.sems[name] = [self.nc.alloc_semaphore(name=name), 0]
        return self.sems[name]

    def n(self, name):
        return self.sem(name)[1]

    def op(self, e, method, *args, wait=(), inc=(), **kw):
        wl = [(self.sem(nm)[0], int(v)) for nm, v in wait]
        il = []
        for nm, k in inc:
            s = self.sem(nm)
            s[1] += k
            il.append((s[0], k))

        def go(eng, method=method, args=args, kw=kw, wl=wl, il=il):
            for h, v in wl:
                eng.wait_ge(h, v)
            ins = getattr(eng, method)(*args, **kw)
            for h, k in il:
                ins.then_inc(h, k)
        self.q[e].append(go)

    def wait(self, e, nm, v):
        h = self.sem(nm)[0]

        def go(eng, h=h, v=int(v)):
            eng.wait_ge(h, v)
        self.q[e].append(go)

    def barrier(self):
        b = self.sem("bar")
        for e in ENGS:
            self.op(e, "nop", inc=(("bar", 1),))
        tgt = b[1]
        for e in ENGS:
            self.wait(e, "bar", tgt)

    def finalize(self):
        with self.nc.Block() as block:
            for e in ENGS:
                fns = self.q[e]

                def run(eng, fns=fns):
                    for go in fns:
                        go(eng)
                getattr(block, e)(run)



def dram_specs():
    s = {
        "x_in": ([B_PER_CORE, T, D], F32),
        "wqT": ([D, D], BF16), "wkT": ([D, D], BF16), "wvT": ([D, D], BF16),
        "woT": ([D, D], BF16),
        "rfT": ([64, 256], BF16),
        "gateT": ([D, 8], BF16),
        "wgT": ([8, D, 512], BF16), "wuT": ([8, D, 512], BF16),
        "wdT": ([8, 512, D], BF16),
        "sgT": ([D, 512], BF16), "suT": ([D, 512], BF16), "sdT": ([512, D], BF16),
        "inwT": ([D, 2048], BF16), "pwT": ([D, 3072], BF16), "owT": ([D, D], BF16),
        "WupT": ([D, 4096], BF16), "WdownT": ([4096, D], BF16),
        "WqT": ([D, 512], BF16), "keysT": ([4, 2, 64, 64], BF16),
        "downT": ([D, 4096], BF16), "upT": ([4096, D], BF16),
        "vecsP": ([128, 104], F32),
        "cw_dram": ([8, 512], BF16),
        "dbg_har": ([128, 36 * 512], BF16),
        "dbg_y": ([128, 4096], F32),
        "xT0": ([B_PER_CORE, D, T], F32),
        "xT1": ([B_PER_CORE, D, T], F32),
        "xT2": ([B_PER_CORE, D, T], F32),
        "xT3": ([B_PER_CORE, D, T], F32),
        "qTb": ([B_PER_CORE, D, T], BF16),
        "kTb": ([B_PER_CORE, D, T], BF16),
        "Vtok": ([B_PER_CORE, 8, T, 128], BF16),
        "scaleQ": ([B_PER_CORE, 16, T], F32),
        "scaleKt": ([B_PER_CORE, T, 16], F32),
        "hoT": ([B_PER_CORE, D, T], BF16),
        "u_pre": ([B_PER_CORE, D, T], F32),
        "sgate": ([B_PER_CORE, D, T], F32),
        "ucf": ([B_PER_CORE, D, T], F32),
        "ucb": ([B_PER_CORE, D, T], BF16),
        "aT": ([B_PER_CORE, D, T], F32),
        "buT": ([B_PER_CORE, D, T], F32),
        "cT": ([B_PER_CORE, D, T], F32),
        "y2b": ([B_PER_CORE, D, T], BF16),
        "dall": ([4096, B_PER_CORE * T], F32),
        "ST": ([4096, B_PER_CORE * T], BF16),
        "out": ([B_PER_CORE, T, D], F32),
    }
    return s


WEIGHT_NAMES = [
    "L0_attn_wq", "L0_attn_wk", "L0_attn_wv", "L0_attn_wo", "L0_attn_rf",
    "L0_ffn_gate_w", "L0_ffn_wg", "L0_ffn_wu", "L0_ffn_wd",
    "L0_ffn_sg", "L0_ffn_su", "L0_ffn_sd",
    "L1_attn_norm_w", "L1_attn_in_w", "L1_attn_in_b", "L1_attn_conv_w",
    "L1_attn_conv_b", "L1_attn_param_w", "L1_attn_param_b", "L1_attn_skip",
    "L1_attn_out_w", "L1_attn_out_b",
    "L1_ffn_W_up", "L1_ffn_W_down", "L1_ffn_W_q", "L1_ffn_keys",
    "L1_ffn_down_embed", "L1_ffn_up_embed",
]

WEIGHT_SHAPES = {
    "L0_attn_wq": [1024, 1024], "L0_attn_wk": [1024, 1024],
    "L0_attn_wv": [1024, 1024], "L0_attn_wo": [1024, 1024],
    "L0_attn_rf": [256, 64],
    "L0_ffn_gate_w": [8, 1024], "L0_ffn_wg": [8, 512, 1024],
    "L0_ffn_wu": [8, 512, 1024], "L0_ffn_wd": [8, 1024, 512],
    "L0_ffn_sg": [512, 1024], "L0_ffn_su": [512, 1024], "L0_ffn_sd": [1024, 512],
    "L1_attn_norm_w": [1024], "L1_attn_in_w": [1024, 2048],
    "L1_attn_in_b": [2048], "L1_attn_conv_w": [4, 1024], "L1_attn_conv_b": [1024],
    "L1_attn_param_w": [1024, 3072], "L1_attn_param_b": [3072],
    "L1_attn_skip": [1024], "L1_attn_out_w": [1024, 1024], "L1_attn_out_b": [1024],
    "L1_ffn_W_up": [4096, 1024], "L1_ffn_W_down": [1024, 4096],
    "L1_ffn_W_q": [512, 1024], "L1_ffn_keys": [4, 64, 2, 64],
    "L1_ffn_down_embed": [4096, 1024], "L1_ffn_up_embed": [4096, 1024],
}


class K:

    def __init__(self, nc, ext_in=(), ext_out=()):
        self.nc = nc
        self.p = Prog(nc)
        self.ext_in = set(ext_in)
        self.ext_out = set(ext_out)
        self._dram = {}
        self._specs = dram_specs()
        self.g = {}
        self._gstack = ExitStack()

    def dram(self, name):
        if name in self._dram:
            return self._dram[name]
        if name in self._specs:
            shape, dt = self._specs[name]
        elif name in WEIGHT_SHAPES:
            shape, dt = WEIGHT_SHAPES[name], F32
        else:
            raise KeyError(name)
        if name in self.ext_in:
            kind = "ExternalInput"
        elif name in self.ext_out:
            kind = "ExternalOutput"
        else:
            kind = "Internal"
        ap = self.nc.dram_tensor(name, shape, dt, kind=kind).ap()
        self._dram[name] = ap
        return ap

    def sb(self, name, shape, dt):
        h = self._gstack.enter_context(self.nc.sbuf_tensor(name, shape, dt))
        self.g[name] = h
        return h

    def stage_stack(self):
        return ExitStack()

    def close(self):
        self._gstack.close()



def build_globals(k: K):
    nc, p = k.nc, k.p
    ident_f = k.sb("ident_f", [128, 128], F32)
    ident_b = k.sb("ident_b", [128, 128], BF16)
    onesc_f = k.sb("onesc_f", [128, 1], F32)
    onesr_f = k.sb("onesr_f", [1, 128], F32)
    onesr_b = k.sb("onesr_b", [1, 128], BF16)
    iota64f = k.sb("iota64f", [128, 64], F32)
    iota8f = k.sb("iota8f", [128, 8], F32)
    iotaPf = k.sb("iotaPf", [128, 1], F32)
    iotaRi = k.sb("iotaRi", [128, 128], I32)
    iotaRf = k.sb("iotaRf", [128, 128], F32)
    iotaPi = k.sb("iotaPi", [128, 1], I32)
    iota64i = k.sb("iota64i", [128, 64], I32)
    iota8i = k.sb("iota8i", [128, 8], I32)
    consts = {}
    for nm, v in [("half_pi", PI / 2), ("neg_half", -0.5), ("neg_ln16", -LN16),
                  ("eps5", 1e-5), ("eps6", 1e-6), ("one", 1.0), ("neg_one", -1.0),
                  ("inv1024", 1.0 / 1024)]:
        consts[nm] = k.sb("c_" + nm, [128, 1], F32)
    k.g["consts"] = consts
    headsel = k.sb("headsel", [128, 128], F32)
    vecs = k.sb("vecs", [128, 104], F32)

    p.op("gpsimd", "iota", iotaRi[:], [[1, 128]], 0, 0)
    p.op("gpsimd", "iota", iotaPi[:], [[0, 1]], 0, 1)
    p.op("gpsimd", "iota", iota64i[:], [[1, 64]], 0, 0)
    p.op("gpsimd", "iota", iota8i[:], [[1, 8]], 0, 0, inc=(("gs", 1),))
    p.wait("vector", "gs", p.n("gs"))
    p.op("vector", "tensor_copy", iotaRf[:], iotaRi[:])
    p.op("vector", "tensor_copy", iotaPf[:], iotaPi[:])
    p.op("vector", "tensor_copy", iota64f[:], iota64i[:])
    p.op("vector", "tensor_copy", iota8f[:], iota8i[:])
    p.op("vector", "tensor_scalar", ident_f[:], iotaRf[:], iotaPf[:], None,
         op0=ALU.is_equal)
    p.op("vector", "tensor_copy", ident_b[:], ident_f[:])
    p.op("vector", "memset", onesc_f[:], 1.0)
    p.op("vector", "memset", onesr_f[:], 1.0)
    p.op("vector", "memset", onesr_b[:], 1.0)
    for nm, v in [("half_pi", PI / 2), ("neg_half", -0.5), ("neg_ln16", -LN16),
                  ("eps5", 1e-5), ("eps6", 1e-6), ("one", 1.0), ("neg_one", -1.0),
                  ("inv1024", 1.0 / 1024)]:
        p.op("vector", "memset", consts[nm][:], v)
    p.op("vector", "memset", headsel[:], 0.0)
    for o in range(8):
        p.op("vector", "memset", headsel[0:64, 16 * o + 2 * o:16 * o + 2 * o + 1], 1.0)
        p.op("vector", "memset", headsel[64:128, 16 * o + 2 * o + 1:16 * o + 2 * o + 2], 1.0)
    p.op("vector", "nop", inc=(("vs", 1),))
    tgt = p.n("vs")
    for e in ENGS:
        p.wait(e, "vs", tgt)


def load_vecs(k: K):
    p = k.p
    p.op("sync", "dma_start", out=k.g["vecs"][:], in_=k.dram("vecsP"),
         inc=(("ld", 16),))
    tgt = p.n("ld")
    for e in ("vector", "scalar", "tensor", "gpsimd"):
        p.wait(e, "ld", tgt)


VECS_COLS = {
    "in_b": (0, 16), "param_b": (16, 24), "out_b": (40, 8), "conv_b": (48, 8),
    "conv_w": (56, 32), "skip": (88, 8), "norm_w": (96, 8),
}


def vcol(k, name, j):
    c0, n = VECS_COLS[name]
    assert 0 <= j < n
    return k.g["vecs"][:, c0 + j:c0 + j + 1]



def s0_prep(k: K):
    nc, p = k.nc, k.p
    st = k.stage_stack()
    src_sb = st.enter_context(nc.sbuf_tensor("s0_src", [128, 512], F32))
    dst_sb = st.enter_context(nc.sbuf_tensor("s0_dst", [128, 512], BF16))
    ps = st.enter_context(nc.psum_tensor("s0_ps", [128, 512], F32))
    vec_sb = st.enter_context(nc.sbuf_tensor("s0_vec", [32, 128], F32))
    vps = st.enter_context(nc.psum_tensor("s0_vps", [128, 32], F32))
    vpack = st.enter_context(nc.sbuf_tensor("s0_vpack", [128, 104], F32))

    def transpose_cast(src_ap, dst_ap, R, C):
        for c0 in range(0, C, 128):
            cw = min(128, C - c0)
            for r0 in range(0, R, 512):
                rn = min(512, R - r0)
                ng = (rn + 127) // 128
                if rn >= 128:
                    src_view = src_ap[r0:r0 + rn, c0:c0 + cw].rearrange(
                        "(g q) c -> q g c", q=128)
                    p.op("sync", "dma_start", out=src_sb[:, 0:ng * cw].rearrange(
                        "q (g c) -> q g c", g=ng), in_=src_view,
                        wait=(("s0c", p.n("s0c")),), inc=(("ld", 16),))
                else:
                    src_view = src_ap[r0:r0 + rn, c0:c0 + cw]
                    p.op("sync", "dma_start", out=src_sb[0:rn, 0:cw], in_=src_view,
                         wait=(("s0c", p.n("s0c")),), inc=(("ld", 16),))
                p.wait("tensor", "ld", p.n("ld"))
                for g in range(ng):
                    gn = min(128, rn - g * 128)
                    p.op("tensor", "transpose", ps[0:cw, g * 128:g * 128 + gn],
                         in_=src_sb[0:gn, g * cw:g * cw + cw],
                         identity=k.g["ident_f"][0:gn, 0:gn],
                         inc=(("ts", 1),) if g == ng - 1 else ())
                p.wait("scalar", "ts", p.n("ts"))
                p.op("scalar", "copy", out=dst_sb[0:cw, 0:rn], in_=ps[0:cw, 0:rn],
                     inc=(("ss", 1),))
                p.wait("sync", "ss", p.n("ss"))
                p.op("sync", "dma_start", out=dst_ap[c0:c0 + cw, r0:r0 + rn],
                     in_=dst_sb[0:cw, 0:rn], inc=(("s0c", 16),))
        p.wait("sync", "s0c", p.n("s0c"))

    transpose_cast(k.dram("L0_attn_wq"), k.dram("wqT"), 1024, 1024)
    transpose_cast(k.dram("L0_attn_wk"), k.dram("wkT"), 1024, 1024)
    transpose_cast(k.dram("L0_attn_wv"), k.dram("wvT"), 1024, 1024)
    transpose_cast(k.dram("L0_attn_wo"), k.dram("woT"), 1024, 1024)
    transpose_cast(k.dram("L0_attn_rf"), k.dram("rfT"), 256, 64)
    transpose_cast(k.dram("L0_ffn_gate_w"), k.dram("gateT"), 8, 1024)
    for e in range(8):
        transpose_cast(k.dram("L0_ffn_wg")[e], k.dram("wgT")[e], 512, 1024)
        transpose_cast(k.dram("L0_ffn_wu")[e], k.dram("wuT")[e], 512, 1024)
        transpose_cast(k.dram("L0_ffn_wd")[e], k.dram("wdT")[e], 1024, 512)
    transpose_cast(k.dram("L0_ffn_sg"), k.dram("sgT"), 512, 1024)
    transpose_cast(k.dram("L0_ffn_su"), k.dram("suT"), 512, 1024)
    transpose_cast(k.dram("L0_ffn_sd"), k.dram("sdT"), 1024, 512)
    transpose_cast(k.dram("L1_ffn_W_up"), k.dram("WupT"), 4096, 1024)
    transpose_cast(k.dram("L1_ffn_W_down"), k.dram("WdownT"), 1024, 4096)
    transpose_cast(k.dram("L1_ffn_W_q"), k.dram("WqT"), 512, 1024)
    transpose_cast(k.dram("L1_ffn_down_embed"), k.dram("downT"), 4096, 1024)
    keys = k.dram("L1_ffn_keys")
    for h in range(4):
        for ph in range(2):
            transpose_cast(keys[h, :, ph, :], k.dram("keysT")[h, ph], 64, 64)

    p.op("gpsimd", "dma_start", out=k.dram("inwT"), in_=k.dram("L1_attn_in_w"),
         inc=(("gd", 16),))
    p.op("gpsimd", "dma_start", out=k.dram("pwT"), in_=k.dram("L1_attn_param_w"),
         inc=(("gd", 16),))
    p.op("gpsimd", "dma_start", out=k.dram("owT"), in_=k.dram("L1_attn_out_w"),
         inc=(("gd", 16),))
    p.op("gpsimd", "dma_start", out=k.dram("upT"), in_=k.dram("L1_ffn_up_embed"),
         inc=(("gd", 16),))

    def pack_vec(src_view, nrows, col0):
        p.op("sync", "dma_start", out=vec_sb[0:nrows, :], in_=src_view,
             wait=(("s0v", p.n("s0v")),), inc=(("ld", 16),))
        p.wait("tensor", "ld", p.n("ld"))
        p.op("tensor", "transpose", vps[:, 0:nrows], in_=vec_sb[0:nrows, :],
             identity=k.g["ident_f"][0:nrows, 0:nrows], inc=(("ts", 1),))
        p.wait("vector", "ts", p.n("ts"))
        p.op("vector", "tensor_copy", out=vpack[:, col0:col0 + nrows],
             in_=vps[:, 0:nrows], inc=(("s0v", 1),))

    pack_vec(k.dram("L1_attn_in_b").rearrange("(g q) -> g q", q=128), 16, 0)
    pack_vec(k.dram("L1_attn_param_b").rearrange("(g q) -> g q", q=128), 24, 16)
    pack_vec(k.dram("L1_attn_out_b").rearrange("(g q) -> g q", q=128), 8, 40)
    pack_vec(k.dram("L1_attn_conv_b").rearrange("(g q) -> g q", q=128), 8, 48)
    pack_vec(k.dram("L1_attn_conv_w").rearrange("w (g q) -> (w g) q", q=128), 32, 56)
    pack_vec(k.dram("L1_attn_skip").rearrange("(g q) -> g q", q=128), 8, 88)
    pack_vec(k.dram("L1_attn_norm_w").rearrange("(g q) -> g q", q=128), 8, 96)
    p.wait("sync", "s0v", p.n("s0v"))
    p.op("sync", "dma_start", out=k.dram("vecsP"), in_=vpack[:],
         inc=(("s0c", 16),))
    p.wait("sync", "s0c", p.n("s0c"))
    p.wait("sync", "gd", p.n("gd"))
    p.barrier()
    st.close()



def s1_ingest(k: K):
    nc, p = k.nc, k.p
    st = k.stage_stack()
    xin = [st.enter_context(nc.sbuf_tensor(f"s1_in{i}", [128, 1024], F32))
           for i in range(2)]
    xev = [st.enter_context(nc.sbuf_tensor(f"s1_ev{i}", [128, 1024], F32))
           for i in range(2)]
    pst = [st.enter_context(nc.psum_tensor(f"s1_ps{i}", [128, 512], F32))
           for i in range(2)]
    x_in = k.dram("x_in")
    xT0 = k.dram("xT0")
    ts0, vs0, st0 = p.n("ts"), p.n("vs"), p.n("st")
    for j in range(B_PER_CORE * NCHUNK):
        b, tt = divmod(j, NCHUNK)
        i = j % 2
        w = (("ts", ts0 + 2 * (j - 1)),) if j >= 2 else ()
        p.op("sync", "dma_start", out=xin[i][:],
             in_=x_in[b, tt * 128:(tt + 1) * 128, :],
             wait=w, inc=(("ld", 16),))
        p.wait("tensor", "ld", p.n("ld"))
        if j >= 1:
            p.wait("tensor", "vs", vs0 + j)
        for half in range(2):
            for g in range(4):
                c = half * 4 + g
                p.op("tensor", "transpose", pst[half][:, g * 128:(g + 1) * 128],
                     in_=xin[i][:, c * 128:(c + 1) * 128],
                     identity=k.g["ident_f"][:],
                     inc=(("ts", 1),) if g == 3 else ())
        base_ts = p.n("ts")
        p.wait("vector", "ts", base_ts - 1)
        if j >= 2:
            p.wait("vector", "st", st0 + 16 * (j - 1))
        p.op("vector", "tensor_copy", out=xev[i][:, 0:512], in_=pst[0][:])
        p.wait("vector", "ts", base_ts)
        p.op("vector", "tensor_copy", out=xev[i][:, 512:1024], in_=pst[1][:],
             inc=(("vs", 1),))
        p.wait("sync", "vs", p.n("vs"))
        dst = xT0[b, :, tt * 128:(tt + 1) * 128].rearrange(
            "(g q) t -> q g t", q=128)
        p.op("sync", "dma_start",
             out=dst, in_=xev[i][:].rearrange("q (g t) -> q g t", g=8),
             inc=(("st", 16),))
    p.wait("sync", "st", p.n("st"))
    p.barrier()
    st.close()



def s2_qkv(k: K):
    nc, p = k.nc, k.p
    st = k.stage_stack()
    swq = st.enter_context(nc.sbuf_tensor("s2_wq", [128, 8 * 1024], BF16))
    swk = st.enter_context(nc.sbuf_tensor("s2_wk", [128, 8 * 1024], BF16))
    swv = st.enter_context(nc.sbuf_tensor("s2_wv", [128, 8 * 1024], BF16))
    sxb = [st.enter_context(nc.sbuf_tensor(f"s2_x{i}", [128, 4096], BF16))
           for i in range(2)]
    sqb = [st.enter_context(nc.sbuf_tensor(f"s2_q{i}", [128, 4096], BF16))
           for i in range(2)]
    skb = [st.enter_context(nc.sbuf_tensor(f"s2_k{i}", [128, 4096], BF16))
           for i in range(2)]
    svb = [st.enter_context(nc.sbuf_tensor(f"s2_v{i}", [128, 4096], BF16))
           for i in range(2)]
    svt = [st.enter_context(nc.sbuf_tensor(f"s2_vt{i}", [128, 4096], BF16))
           for i in range(2)]
    ssq = st.enter_context(nc.sbuf_tensor("s2_sq", [128, 512], F32))
    sscl = [st.enter_context(nc.sbuf_tensor(f"s2_scl{i}", [16, 1024], F32))
            for i in range(2)]
    psq = [st.enter_context(nc.psum_tensor(f"s2_ps{i}", [128, 512], F32))
           for i in range(2)]
    psn = st.enter_context(nc.psum_tensor("s2_psn", [16, 512], F32))
    psvt = st.enter_context(nc.psum_tensor("s2_psvt", [128, 512], BF16))
    psst = st.enter_context(nc.psum_tensor("s2_psst", [128, 64], F32))
    ssclt = [st.enter_context(nc.sbuf_tensor(f"s2_sclt{i}", [128, 64], F32))
             for i in range(2)]

    for nm, sb in (("wqT", swq), ("wkT", swk), ("wvT", swv)):
        p.op("sync", "dma_start", out=sb[:].rearrange("q (g o) -> q g o", g=8),
             in_=k.dram(nm).rearrange("(g q) o -> q g o", q=128),
             inc=(("ld", 16),))
    wready = p.n("ld")
    p.wait("tensor", "ld", wready)

    xT0 = k.dram("xT0")
    st0 = p.n("st")
    for j in range(B_PER_CORE * NTILE):
        b, ti = divmod(j, NTILE)
        t0 = ti * TT
        i = j % 2
        w = (("s2t", j - 1),) if j >= 2 else ()
        p.op("gpsimd", "dma_start", out=sxb[i][:].rearrange("q (g t) -> q g t", g=8),
             in_=xT0[b, :, t0:t0 + TT].rearrange("(g q) t -> q g t", q=128),
             wait=w, inc=(("gd", 16),))
        gd_x = p.n("gd")
        p.wait("tensor", "gd", gd_x)
        if j >= 2:
            p.wait("scalar", "st", st0 + 192 * (j - 1))

        def proj(warena, out_arena, scale_dst):
            p.wait("tensor", "ss", p.n("ss"))
            for o in range(8):
                bank = psq[o % 2]
                if o >= 2:
                    p.wait("tensor", "ss", p.n("ss") - 1)
                for g in range(8):
                    p.op("tensor", "matmul", bank[:],
                         lhsT=warena[:, g * 1024 + o * 128:g * 1024 + (o + 1) * 128],
                         rhs=sxb[i][:, g * TT:(g + 1) * TT],
                         start=(g == 0), stop=(g == 7),
                         inc=(("ts", 1),) if g == 7 else ())
                p.wait("scalar", "ts", p.n("ts"))
                p.op("scalar", "copy", out=out_arena[:, o * TT:(o + 1) * TT],
                     in_=bank[:])
                p.op("scalar", "activation", ssq[:], bank[:], AF.Square,
                     inc=(("ss", 2),))
                p.wait("tensor", "ss", p.n("ss"))
                p.op("tensor", "matmul", psn[:],
                     lhsT=k.g["headsel"][:, 16 * o:16 * (o + 1)], rhs=ssq[:],
                     start=(o == 0), stop=(o == 7),
                     inc=(("ts", 1),) if o == 7 else ())
            p.wait("scalar", "ts", p.n("ts"))
            p.op("scalar", "activation", scale_dst, psn[:], AF.Exp,
                 bias=k.g["consts"]["neg_ln16"][0:16, :],
                 scale=k.g["consts"]["neg_half"][0:16, :], inc=(("ss", 1),))

        proj(swq, sqb[i], sscl[i][:, 0:512])
        proj(swk, skb[i], sscl[i][:, 512:1024])
        p.wait("tensor", "ss", p.n("ss"))
        for tj in range(4):
            p.op("tensor", "transpose", psst[:, tj * 16:(tj + 1) * 16],
                 in_=sscl[i][0:16, 512 + tj * 128:512 + (tj + 1) * 128],
                 identity=k.g["ident_f"][0:16, 0:16],
                 inc=(("ts", 1),) if tj == 3 else ())
        p.wait("scalar", "ts", p.n("ts"))
        p.op("scalar", "copy", out=ssclt[i][:], in_=psst[:], inc=(("ss", 1),))
        p.wait("tensor", "ss", p.n("ss"))
        for o in range(8):
            bank = psq[o % 2]
            if o >= 2:
                p.wait("tensor", "ss", p.n("ss") - 1)
            for g in range(8):
                p.op("tensor", "matmul", bank[:],
                     lhsT=swv[:, g * 1024 + o * 128:g * 1024 + (o + 1) * 128],
                     rhs=sxb[i][:, g * TT:(g + 1) * TT],
                     start=(g == 0), stop=(g == 7),
                     inc=(("ts", 1),) if g == 7 else ())
            p.wait("scalar", "ts", p.n("ts"))
            p.op("scalar", "copy", out=svb[i][:, o * TT:(o + 1) * TT],
                 in_=bank[:], inc=(("ss", 1),))
            p.wait("tensor", "ss", p.n("ss"))
            for tj in range(4):
                p.op("tensor", "transpose",
                     psvt[:, tj * 128:(tj + 1) * 128],
                     in_=svb[i][:, o * TT + tj * 128:o * TT + (tj + 1) * 128],
                     identity=k.g["ident_b"][:],
                     inc=(("ts", 1),) if tj == 3 else ())
            p.wait("scalar", "ts", p.n("ts"))
            p.op("scalar", "copy", out=svt[i][:, o * TT:(o + 1) * TT],
                 in_=psvt[:], inc=(("ss", 1),))
        p.wait("sync", "ss", p.n("ss"))
        p.op("sync", "dma_start",
             out=k.dram("qTb")[b, :, t0:t0 + TT].rearrange("(g q) t -> q g t", q=128),
             in_=sqb[i][:].rearrange("q (g t) -> q g t", g=8), inc=(("st", 16),))
        p.op("sync", "dma_start",
             out=k.dram("kTb")[b, :, t0:t0 + TT].rearrange("(g q) t -> q g t", q=128),
             in_=skb[i][:].rearrange("q (g t) -> q g t", g=8), inc=(("st", 16),))
        p.op("sync", "dma_start", out=k.dram("scaleQ")[b, :, t0:t0 + TT],
             in_=sscl[i][:, 0:512], inc=(("st", 16),))
        p.op("sync", "dma_start",
             out=k.dram("scaleKt")[b, t0:t0 + TT, :].rearrange(
                 "(tj q) s -> q tj s", q=128),
             in_=ssclt[i][:].rearrange("q (tj s) -> q tj s", tj=4),
             inc=(("st", 16),))
        for o in range(8):
            p.op("sync", "dma_start",
                 out=k.dram("Vtok")[b, o, t0:t0 + TT, :].rearrange(
                     "(tj q) f -> q tj f", q=128),
                 in_=svt[i][:, o * TT:(o + 1) * TT].rearrange(
                     "q (tj f) -> q tj f", tj=4),
                 inc=(("st", 16),))
        p.op("sync", "nop", inc=(("s2t", 1),))
    p.wait("sync", "st", p.n("st"))
    p.barrier()
    st.close()



def emit_trig(k, src_ap, t1, t2, dst_cos, dst_sin, ss_guard=None):
    p = k.p
    if ss_guard is not None:
        p.wait("vector", "ss", ss_guard)
    p.op("vector", "tensor_scalar", t1, src_ap, 1.0 / (2 * PI), MAGIC,
         op0=ALU.mult, op1=ALU.add)
    p.op("vector", "tensor_scalar", t1, t1, MAGIC, None, op0=ALU.subtract)
    p.op("vector", "scalar_tensor_tensor", t1, t1, -2 * PI, src_ap,
         op0=ALU.mult, op1=ALU.add, inc=(("vs", 1),))
    p.wait("scalar", "vs", p.n("vs"))
    p.op("scalar", "activation", dst_sin, t1, AF.Sin)
    p.op("scalar", "activation", t2, t1, AF.Abs)
    p.op("scalar", "activation", dst_cos, t2, AF.Sin,
         bias=k.g["consts"]["half_pi"][:], scale=k.g["consts"]["neg_one"][:],
         inc=(("ss", 1),))


def s3_attn(k: K):
    nc, p = k.nc, k.p
    st = k.stage_stack()
    srf = st.enter_context(nc.sbuf_tensor("s3_rf", [64, 256], BF16))
    skb = [st.enter_context(nc.sbuf_tensor(f"s3_k{i}", [64, 4096], BF16))
           for i in range(2)]
    sqb = [st.enter_context(nc.sbuf_tensor(f"s3_q{i}", [64, 4096], BF16))
           for i in range(2)]
    svp = [st.enter_context(nc.sbuf_tensor(f"s3_v{i}", [128, 4096], BF16))
           for i in range(2)]
    ssct = [st.enter_context(nc.sbuf_tensor(f"s3_sct{i}", [128, 512], F32))
            for i in range(2)]
    ssqr = [st.enter_context(nc.sbuf_tensor(f"s3_sqr{i}", [1, 4096], F32))
            for i in range(2)]
    kf = st.enter_context(nc.sbuf_tensor("s3_kfa", [128, 32 * 512], BF16))
    srhs = st.enter_context(nc.sbuf_tensor("s3_rhsa", [128, 32 * 72], BF16))
    tA1 = [st.enter_context(nc.sbuf_tensor(f"s3_tA1{i}", [128, 256], F32))
           for i in range(2)]
    tA2 = [st.enter_context(nc.sbuf_tensor(f"s3_tA2{i}", [128, 256], F32))
           for i in range(2)]
    skv = st.enter_context(nc.sbuf_tensor("s3_kv", [128, 260], BF16))
    qf = [st.enter_context(nc.sbuf_tensor(f"s3_qf{i}", [128, 512], BF16))
          for i in range(4)]
    tB1 = st.enter_context(nc.sbuf_tensor("s3_tB1", [128, 512], F32))
    tB2 = st.enter_context(nc.sbuf_tensor("s3_tB2", [128, 512], F32))
    srow = st.enter_context(nc.sbuf_tensor("s3_row", [1, 512 * 4], F32))
    sgb = st.enter_context(nc.sbuf_tensor("s3_gb", [64, 512], F32))
    sho = st.enter_context(nc.sbuf_tensor("s3_ho", [64, 512], BF16))
    pbank = [st.enter_context(nc.psum_tensor(f"s3_pb{i}", [128, 512], F32))
             for i in range(8)]
    psA = [pbank[0][:, 0:256], pbank[1][:, 0:256]]
    psKV = [pbank[2][:, 0:65], pbank[3][:, 0:65], pbank[4][:, 0:65],
            pbank[5][:, 0:65]]
    psB = [pbank[0], pbank[1]]
    psO = pbank[6][0:65, :]
    psG = pbank[7][0:64, :]

    p.op("sync", "dma_start", out=srf[:], in_=k.dram("rfT"), inc=(("ld", 16),))
    p.wait("tensor", "ld", p.n("ld"))

    for it in range(B_PER_CORE * 16):
        b, h = divmod(it, 16)
        i2 = it % 2
        hp = h % 2
        w = (("s3i", it - 1),) if it >= 2 else ()
        p.op("sync", "dma_start", out=skb[i2][:],
             in_=k.dram("kTb")[b, 64 * h:64 * (h + 1), :], wait=w,
             inc=(("ld", 16),))
        p.op("sync", "dma_start", out=sqb[i2][:],
             in_=k.dram("qTb")[b, 64 * h:64 * (h + 1), :], inc=(("ld", 16),))
        p.op("sync", "dma_start",
             out=ssct[i2][:].rearrange("q (tj s) -> q tj s", s=16),
             in_=k.dram("scaleKt")[b].rearrange("(tj q) s -> q tj s", q=128),
             inc=(("ld", 16),))
        p.op("sync", "dma_start", out=ssqr[i2][:],
             in_=k.dram("scaleQ")[b, h:h + 1, :], inc=(("ld", 16),))
        if hp == 0:
            p.op("sync", "dma_start",
                 out=svp[i2][:].rearrange("q (tj f) -> q tj f", f=128),
                 in_=k.dram("Vtok")[b, h // 2].rearrange("(tj q) f -> q tj f", q=128),
                 inc=(("ld", 16),))
        ld_here = p.n("ld")
        p.wait("tensor", "ld", ld_here)
        p.wait("vector", "ld", ld_here)
        p.wait("vector", "ts", p.n("ts"))
        iv = i2 if hp == 0 else 1 - i2
        p.wait("tensor", "ss", p.n("ss"))
        for tt in range(NCHUNK):
            ic = tt % 2
            kslot = kf[:, tt * 512:(tt + 1) * 512]
            rslot = srhs[:, tt * 72:tt * 72 + 72]
            scol = ssct[i2][:, tt * 16 + h:tt * 16 + h + 1]
            p.op("vector", "tensor_scalar",
                 rslot[:, 0:64],
                 svp[iv][:, tt * 128 + hp * 64:tt * 128 + hp * 64 + 64],
                 scol, None, op0=ALU.mult)
            p.op("vector", "tensor_copy", out=rslot[:, 64:65], in_=scol,
                 inc=(("vs", 1),))
            vs_rhs = p.n("vs")
            if tt >= 2:
                p.wait("tensor", "vs", vs_rhs - 1)
            p.op("tensor", "matmul", psA[ic],
                 lhsT=skb[i2][:, tt * 128:(tt + 1) * 128], rhs=srf[:],
                 start=True, stop=True, inc=(("ts", 1),))
            p.wait("vector", "ts", p.n("ts"))
            emit_trig(k, psA[ic], tA1[ic][:], tA2[ic][:],
                      kslot[:, 0:256], kslot[:, 256:512],
                      ss_guard=p.n("ss") - 1)
            p.wait("tensor", "ss", p.n("ss"))
            p.wait("tensor", "vs", vs_rhs)
            p.op("tensor", "matmul", psKV[0],
                 lhsT=kslot[:, 0:128], rhs=rslot[:, 0:65],
                 start=(tt == 0), stop=(tt == NCHUNK - 1),
                 inc=(("ts", 1),))
        for jj in range(1, 4):
            for tt in range(NCHUNK):
                p.op("tensor", "matmul", psKV[jj],
                     lhsT=kf[:, tt * 512 + jj * 128:tt * 512 + (jj + 1) * 128],
                     rhs=srhs[:, tt * 72:tt * 72 + 65],
                     start=(tt == 0), stop=(tt == NCHUNK - 1),
                     inc=(("ts", 1),) if tt == NCHUNK - 1 else ())
        p.wait("scalar", "ts", p.n("ts"))
        for jj in range(4):
            p.op("scalar", "copy", out=skv[:, jj * 65:(jj + 1) * 65],
                 in_=psKV[jj], inc=(("ss", 1),) if jj == 3 else ())
        for tb in range(NTILE):
            p.wait("tensor", "vs", p.n("vs"))
            for mi in range(2):
                p.op("tensor", "matmul", psB[mi][:],
                     lhsT=srf[:, mi * 128:(mi + 1) * 128],
                     rhs=sqb[i2][:, tb * 512:(tb + 1) * 512],
                     start=True, stop=True, inc=(("ts", 1),))
            p.wait("vector", "ts", p.n("ts"))
            for mi in range(2):
                emit_trig(k, psB[mi][:], tB1[:], tB2[:],
                          qf[mi][:], qf[2 + mi][:], ss_guard=p.n("ss"))
            p.wait("tensor", "ss", p.n("ss"))
            for jj in range(4):
                p.op("tensor", "matmul", psO,
                     lhsT=skv[:, jj * 65:(jj + 1) * 65], rhs=qf[jj][:],
                     start=(jj == 0), stop=(jj == 3),
                     inc=(("ts", 1),) if jj == 3 else ())
            p.wait("vector", "ts", p.n("ts"))
            sq_row = ssqr[i2][0:1, tb * 512:(tb + 1) * 512]
            p.op("vector", "tensor_tensor", srow[0:1, 0:512], psO[64:65, :],
                 sq_row, op=ALU.mult, inc=(("vs", 1),))
            p.wait("vector", "vs", p.n("vs"))
            p.op("vector", "tensor_scalar", srow[0:1, 512:1024],
                 srow[0:1, 0:512], 1e-6, None, op0=ALU.max, inc=(("vs", 1),))
            p.wait("vector", "vs", p.n("vs"))
            p.op("vector", "reciprocal", srow[0:1, 1024:1536],
                 srow[0:1, 512:1024], inc=(("vs", 1),))
            p.wait("vector", "vs", p.n("vs"))
            p.op("vector", "tensor_tensor", srow[0:1, 1536:2048], sq_row,
                 srow[0:1, 1024:1536], op=ALU.mult, inc=(("vs", 1),))
            p.wait("tensor", "vs", p.n("vs"))
            p.op("tensor", "matmul", psG, lhsT=k.g["onesr_f"][:, 0:64],
                 rhs=srow[0:1, 1536:2048], start=True, stop=True,
                 inc=(("ts", 1),))
            p.wait("scalar", "ts", p.n("ts"))
            p.op("scalar", "copy", out=sgb[:], in_=psG, inc=(("ss", 1),))
            p.wait("vector", "ss", p.n("ss"))
            p.wait("vector", "st", p.n("st"))
            p.op("vector", "tensor_tensor", sho[:], psO[0:64, :], sgb[:],
                 op=ALU.mult, inc=(("vs", 1),))
            p.wait("sync", "vs", p.n("vs"))
            p.op("sync", "dma_start",
                 out=k.dram("hoT")[b, 64 * h:64 * (h + 1), tb * 512:(tb + 1) * 512],
                 in_=sho[:], inc=(("st", 16),))
        p.op("sync", "nop", wait=(("st", p.n("st")),), inc=(("s3i", 1),))
    p.wait("sync", "s3i", p.n("s3i"))
    p.barrier()
    st.close()



def emit_ln(k, ysb, ssq, rows, psm, pss, psr, psmr, tmp, xout):
    p = k.p
    for o in range(8):
        p.op("tensor", "matmul", psm, lhsT=k.g["onesc_f"][:],
             rhs=ysb[:, o * 512:(o + 1) * 512], start=(o == 0), stop=(o == 7))
    for o in range(8):
        p.op("scalar", "activation", ssq[:, o * 512:(o + 1) * 512],
             ysb[:, o * 512:(o + 1) * 512], AF.Square,
             inc=(("ss", 1),) if o == 7 else ())
    p.wait("tensor", "ss", p.n("ss"))
    for o in range(8):
        p.op("tensor", "matmul", pss, lhsT=k.g["onesc_f"][:],
             rhs=ssq[:, o * 512:(o + 1) * 512], start=(o == 0), stop=(o == 7),
             inc=(("ts", 1),) if o == 7 else ())
    p.wait("vector", "ts", p.n("ts"))
    p.op("vector", "tensor_scalar", rows[0:1, 0:512], psm, 1.0 / 1024, None,
         op0=ALU.mult, inc=(("vs", 1),))
    p.wait("vector", "vs", p.n("vs"))
    p.op("vector", "tensor_tensor", rows[0:1, 512:1024], rows[0:1, 0:512],
         rows[0:1, 0:512], op=ALU.mult, inc=(("vs", 1),))
    p.wait("vector", "vs", p.n("vs"))
    p.op("vector", "scalar_tensor_tensor", rows[0:1, 512:1024], pss,
         1.0 / 1024, rows[0:1, 512:1024], op0=ALU.mult, op1=ALU.subtract,
         inc=(("vs", 1),))
    p.wait("scalar", "vs", p.n("vs"))
    p.op("scalar", "activation", rows[0:1, 1024:1536], rows[0:1, 512:1024],
         AF.Ln, bias=k.g["consts"]["eps5"][0:1, :], inc=(("ss", 1),))
    p.wait("scalar", "ss", p.n("ss"))
    p.op("scalar", "activation", rows[0:1, 1024:1536], rows[0:1, 1024:1536],
         AF.Exp, scale=k.g["consts"]["neg_half"][0:1, :], inc=(("ss", 1),))
    p.wait("vector", "ss", p.n("ss"))
    p.op("vector", "tensor_tensor", rows[0:1, 1536:2048], rows[0:1, 0:512],
         rows[0:1, 1024:1536], op=ALU.mult, inc=(("vs", 1),))
    p.wait("tensor", "vs", p.n("vs"))
    p.op("tensor", "matmul", psr, lhsT=k.g["onesr_f"][:],
         rhs=rows[0:1, 1024:1536], start=True, stop=True)
    p.op("tensor", "matmul", psmr, lhsT=k.g["onesr_f"][:],
         rhs=rows[0:1, 1536:2048], start=True, stop=True, inc=(("ts", 1),))
    p.wait("vector", "ts", p.n("ts"))
    for o in range(8):
        p.op("vector", "tensor_tensor", tmp[:], ysb[:, o * 512:(o + 1) * 512],
             psr, op=ALU.mult)
        p.op("vector", "tensor_tensor", xout[:, o * 512:(o + 1) * 512],
             tmp[:], psmr, op=ALU.subtract,
             inc=(("vs", 1),) if o == 7 else ())



def s4_wo_ln(k: K):
    nc, p = k.nc, k.p
    st = k.stage_stack()
    swo = st.enter_context(nc.sbuf_tensor("s4_wo", [128, 8 * 1024], BF16))
    hob = [st.enter_context(nc.sbuf_tensor(f"s4_ho{i}", [128, 4096], BF16))
           for i in range(2)]
    x0 = [st.enter_context(nc.sbuf_tensor(f"s4_x0{i}", [128, 4096], F32))
          for i in range(2)]
    ysb = st.enter_context(nc.sbuf_tensor("s4_y", [128, 4096], F32))
    ssq = st.enter_context(nc.sbuf_tensor("s4_sq", [128, 4096], F32))
    xout = st.enter_context(nc.sbuf_tensor("s4_xo", [128, 4096], F32))
    rows = st.enter_context(nc.sbuf_tensor("s4_rows", [1, 2048], F32))
    tmp = st.enter_context(nc.sbuf_tensor("s4_tmp", [128, 512], F32))
    pb = [st.enter_context(nc.psum_tensor(f"s4_pb{i}", [128, 512], F32))
          for i in range(4)]
    psm = pb[2][0:1, 0:512]
    pss = pb[2][0:1, 0:512]
    psm = pb[2][0:1, :]
    pss = pb[3][0:1, :]
    psr = st.enter_context(nc.psum_tensor("s4_psr", [128, 512], F32))
    psmr = st.enter_context(nc.psum_tensor("s4_psmr", [128, 512], F32))

    p.op("sync", "dma_start", out=swo[:].rearrange("q (g o) -> q g o", g=8),
         in_=k.dram("woT").rearrange("(g q) o -> q g o", q=128),
         inc=(("ld", 16),))
    p.wait("tensor", "ld", p.n("ld"))

    for j in range(B_PER_CORE * NTILE):
        b, ti = divmod(j, NTILE)
        t0 = ti * TT
        i = j % 2
        w = (("s4i", j - 1),) if j >= 2 else ()
        p.op("sync", "dma_start", out=hob[i][:].rearrange("q (g t) -> q g t", g=8),
             in_=k.dram("hoT")[b, :, t0:t0 + TT].rearrange("(g q) t -> q g t", q=128),
             wait=w, inc=(("ld", 16),))
        p.op("sync", "dma_start", out=x0[i][:].rearrange("q (g t) -> q g t", g=8),
             in_=k.dram("xT0")[b, :, t0:t0 + TT].rearrange("(g q) t -> q g t", q=128),
             inc=(("ld", 16),))
        p.wait("tensor", "ld", p.n("ld"))
        p.wait("vector", "ld", p.n("ld"))
        p.wait("tensor", "vs", p.n("vs"))
        for o in range(8):
            bank = pb[o % 2]
            if o >= 2:
                p.wait("tensor", "vs", p.n("vs") - 1)
            for g in range(8):
                p.op("tensor", "matmul", bank[:],
                     lhsT=swo[:, g * 1024 + o * 128:g * 1024 + (o + 1) * 128],
                     rhs=hob[i][:, g * TT:(g + 1) * TT],
                     start=(g == 0), stop=(g == 7),
                     inc=(("ts", 1),) if g == 7 else ())
            p.wait("vector", "ts", p.n("ts"))
            p.op("vector", "tensor_tensor", ysb[:, o * TT:(o + 1) * TT],
                 bank[:], x0[i][:, o * TT:(o + 1) * TT], op=ALU.add,
                 inc=(("vs", 1),))
        p.wait("scalar", "vs", p.n("vs"))
        p.wait("tensor", "vs", p.n("vs"))
        p.wait("vector", "st", p.n("st"))
        emit_ln(k, ysb, ssq, rows, psm, pss, psr[:], psmr[:], tmp, xout)
        p.wait("sync", "vs", p.n("vs"))
        p.op("sync", "dma_start",
             out=k.dram("xT1")[b, :, t0:t0 + TT].rearrange("(g q) t -> q g t", q=128),
             in_=xout[:].rearrange("q (g t) -> q g t", g=8),
             inc=(("st", 16),))
        p.op("sync", "nop", wait=(("st", p.n("st")),), inc=(("s4i", 1),))
    p.wait("sync", "s4i", p.n("s4i"))
    p.barrier()
    st.close()



def s5_moe(k: K):
    nc, p = k.nc, k.p
    st = k.stage_stack()
    sgt = st.enter_context(nc.sbuf_tensor("s5_gt", [128, 64], BF16))
    ssg = st.enter_context(nc.sbuf_tensor("s5_sg", [128, 8 * 512], BF16))
    ssu = st.enter_context(nc.sbuf_tensor("s5_su", [128, 8 * 512], BF16))
    ssd = st.enter_context(nc.sbuf_tensor("s5_sd", [128, 4 * 1024], BF16))
    sxb = [st.enter_context(nc.sbuf_tensor(f"s5_x{i}", [128, 4096], BF16))
           for i in range(2)]
    x1f = st.enter_context(nc.sbuf_tensor("s5_x1r", [128, 4096], F32))
    swg = [st.enter_context(nc.sbuf_tensor(f"s5_wg{i}", [128, 8 * 512], BF16))
           for i in range(2)]
    swu = [st.enter_context(nc.sbuf_tensor(f"s5_wu{i}", [128, 8 * 512], BF16))
           for i in range(2)]
    swd = [st.enter_context(nc.sbuf_tensor(f"s5_wd{i}", [128, 4 * 128], BF16))
           for i in range(2)]
    sg8 = st.enter_context(nc.sbuf_tensor("s5_g8", [8, 512], F32))
    stok = st.enter_context(nc.sbuf_tensor("s5_tok", [128, 32], F32))
    swk = st.enter_context(nc.sbuf_tensor("s5_wk", [128, 32], F32))
    se8 = st.enter_context(nc.sbuf_tensor("s5_e8", [128, 8], F32))
    sm8 = st.enter_context(nc.sbuf_tensor("s5_m8", [128, 8], F32))
    cwtok = st.enter_context(nc.sbuf_tensor("s5_cwt", [128, 32], F32))
    scwT = st.enter_context(nc.sbuf_tensor("s5_cwT", [8, 512], F32))
    scw1 = st.enter_context(nc.sbuf_tensor("s5_cw1", [1, 4096], BF16))
    ssil = st.enter_context(nc.sbuf_tensor("s5_sil", [128, 512], F32))
    sh1 = st.enter_context(nc.sbuf_tensor("s5_h1", [128, 512], F32))
    har = st.enter_context(nc.sbuf_tensor("s5_har", [128, 36 * 512], BF16))
    ysb = st.enter_context(nc.sbuf_tensor("s5_y", [128, 4096], F32))
    xout = st.enter_context(nc.sbuf_tensor("s5_xo", [128, 4096], F32))
    ssq = xout
    rows = st.enter_context(nc.sbuf_tensor("s5_rows", [1, 2048], F32))
    tmp = st.enter_context(nc.sbuf_tensor("s5_tmp", [128, 512], F32))
    pb = [st.enter_context(nc.psum_tensor(f"s5_pb{i}", [128, 512], F32))
          for i in range(8)]
    psm = pb[4][0:1, :]
    pss = pb[5][0:1, :]

    p.op("sync", "dma_start", out=sgt[:].rearrange("q (g o) -> q g o", g=8),
         in_=k.dram("gateT").rearrange("(g q) o -> q g o", q=128),
         inc=(("ld", 16),))
    p.op("sync", "dma_start", out=ssg[:].rearrange("q (g f) -> q g f", g=8),
         in_=k.dram("sgT").rearrange("(g q) f -> q g f", q=128),
         inc=(("ld", 16),))
    p.op("sync", "dma_start", out=ssu[:].rearrange("q (g f) -> q g f", g=8),
         in_=k.dram("suT").rearrange("(g q) f -> q g f", q=128),
         inc=(("ld", 16),))
    p.op("sync", "dma_start", out=ssd[:].rearrange("q (g o) -> q g o", g=4),
         in_=k.dram("sdT").rearrange("(g q) o -> q g o", q=128),
         inc=(("ld", 16),))
    p.wait("tensor", "ld", p.n("ld"))

    def load_gu(e, we, guard):
        p.op("sync", "dma_start", out=swg[we][:].rearrange("q (g f) -> q g f", g=8),
             in_=k.dram("wgT")[e].rearrange("(g q) f -> q g f", q=128),
             wait=guard, inc=(("ld", 16),))
        p.op("sync", "dma_start", out=swu[we][:].rearrange("q (g f) -> q g f", g=8),
             in_=k.dram("wuT")[e].rearrange("(g q) f -> q g f", q=128),
             inc=(("ld", 16),))

    def load_wd(e, we, guard):
        p.op("sync", "dma_start", out=swd[we][:].rearrange("q (g o) -> q g o", g=4),
             in_=k.dram("wdT")[e].rearrange("(g q) o -> q g o", q=128),
             wait=guard, inc=(("ld", 16),))

    for j in range(B_PER_CORE * NTILE):
        b, ti = divmod(j, NTILE)
        t0 = ti * TT
        i = j % 2
        w = (("s5i", j - 1),) if j >= 2 else ()
        p.op("gpsimd", "dma_start", out=sxb[i][:].rearrange("q (g t) -> q g t", g=8),
             in_=k.dram("xT1")[b, :, t0:t0 + TT].rearrange("(g q) t -> q g t", q=128),
             wait=w, inc=(("gd", 16),))
        wx = (("s5i", j),) if j >= 1 else ()
        p.op("sync", "dma_start", out=x1f[:].rearrange("q (g t) -> q g t", g=8),
             in_=k.dram("xT1")[b, :, t0:t0 + TT].rearrange("(g q) t -> q g t", q=128),
             wait=wx, inc=(("ld", 16),))
        wg_guard = (("s5g", p.n("s5g")),)
        load_gu(0, 0, wg_guard)
        load_gu(1, 1, ())
        p.wait("tensor", "gd", p.n("gd"))
        p.wait("tensor", "vs", p.n("vs"))
        gv = pb[3][0:8, :]
        for g in range(8):
            p.op("tensor", "matmul", gv, lhsT=sgt[:, g * 8:(g + 1) * 8],
                 rhs=sxb[i][:, g * TT:(g + 1) * TT],
                 start=(g == 0), stop=(g == 7),
                 inc=(("ts", 1),) if g == 7 else ())
        p.wait("scalar", "ts", p.n("ts"))
        p.op("scalar", "copy", out=sg8[:], in_=gv, inc=(("ss", 1),))
        p.wait("tensor", "ss", p.n("ss"))
        for c in range(4):
            p.op("tensor", "transpose", pb[3][:, 32 + c * 8:32 + (c + 1) * 8],
                 in_=sg8[:, c * 128:(c + 1) * 128],
                 identity=k.g["ident_f"][0:8, 0:8],
                 inc=(("ts", 1),) if c == 3 else ())
        p.wait("vector", "ts", p.n("ts"))
        p.op("vector", "tensor_copy", out=stok[:], in_=pb[3][:, 32:64],
             inc=(("vs", 1),))
        for c in range(4):
            sl = stok[:, 8 * c:8 * (c + 1)]
            p.wait("vector", "vs", p.n("vs"))
            p.op("vector", "reduce_max", swk[:, 0:1], sl,
                 axis=mybir.AxisListType.X, inc=(("vs", 1),))
            p.wait("vector", "vs", p.n("vs"))
            p.op("vector", "tensor_scalar", swk[:, 1:2], swk[:, 0:1], -1.0, None,
                 op0=ALU.mult, inc=(("vs", 1),))
            p.wait("scalar", "vs", p.n("vs"))
            p.op("scalar", "activation", se8[:], sl, AF.Exp, bias=swk[:, 1:2],
                 inc=(("ss", 1),))
            p.wait("vector", "ss", p.n("ss"))
            p.op("vector", "reduce_sum", swk[:, 2:3], se8[:],
                 axis=mybir.AxisListType.X, inc=(("vs", 1),))
            p.wait("vector", "vs", p.n("vs"))
            p.op("vector", "reciprocal", swk[:, 3:4], swk[:, 2:3],
                 inc=(("vs", 1),))
            p.wait("vector", "vs", p.n("vs"))
            p.op("vector", "tensor_scalar", se8[:], se8[:], swk[:, 3:4], None,
                 op0=ALU.mult, inc=(("vs", 1),))
            p.wait("vector", "vs", p.n("vs"))
            p.op("vector", "max", out=sm8[:], in_=se8[:], inc=(("vs", 1),))
            p.wait("vector", "vs", p.n("vs"))
            p.op("vector", "scalar_tensor_tensor", cwtok[:, 8 * c:8 * (c + 1)],
                 se8[:], sm8[:, 1:2], se8[:], op0=ALU.is_ge, op1=ALU.mult,
                 inc=(("vs", 1),))
        p.wait("tensor", "vs", p.n("vs"))
        for c in range(4):
            p.op("tensor", "transpose", pb[4][0:8, c * 128:(c + 1) * 128],
                 in_=cwtok[:, 8 * c:8 * (c + 1)],
                 identity=k.g["ident_f"][:],
                 inc=(("ts", 1),) if c == 3 else ())
        p.wait("scalar", "ts", p.n("ts"))
        p.op("scalar", "copy", out=scwT[:], in_=pb[4][0:8, :], inc=(("ss", 1),))
        p.wait("gpsimd", "ss", p.n("ss"))
        p.op("gpsimd", "dma_start", out=k.dram("cw_dram"), in_=scwT[:],
             inc=(("gd", 16),))
        p.wait("gpsimd", "gd", p.n("gd"))
        p.op("gpsimd", "dma_start", out=scw1[:],
             in_=k.dram("cw_dram").rearrange("e t -> (e t)").unsqueeze(0),
             inc=(("gd", 16),))
        scw_ld = p.n("gd")
        scw_ss = p.n("ss")
        for e in range(9):
            we = e % 2
            ld_need = p.n("ld")
            if e < 8:
                p.wait("tensor", "gd", scw_ld)
                if e >= 1:
                    p.wait("tensor", "vs", p.n("vs"))
                p.op("tensor", "matmul", pb[2][:], lhsT=k.g["onesr_b"][:],
                     rhs=scw1[0:1, e * 512:(e + 1) * 512], start=True, stop=True,
                     inc=(("ts", 1),))
            if e + 2 < 8:
                load_gu(e + 2, (e + 2) % 2, (("s5g", p.n("s5g") + 1),))
            if e < 8:
                p.wait("tensor", "ld", ld_need)
            for f in range(4):
                wga = swg[we] if e < 8 else ssg
                wua = swu[we] if e < 8 else ssu
                p.wait("tensor", "ss", p.n("ss"))
                p.wait("tensor", "vs", p.n("vs"))
                for g in range(8):
                    p.op("tensor", "matmul", pb[0][:],
                         lhsT=wga[:, g * 512 + f * 128:g * 512 + (f + 1) * 128],
                         rhs=sxb[i][:, g * TT:(g + 1) * TT],
                         start=(g == 0), stop=(g == 7),
                         inc=(("ts", 1),) if g == 7 else ())
                p.wait("scalar", "ts", p.n("ts"))
                p.op("scalar", "activation", ssil[:], pb[0][:], AF.Silu,
                     inc=(("ss", 1),))
                for g in range(8):
                    p.op("tensor", "matmul", pb[1][:],
                         lhsT=wua[:, g * 512 + f * 128:g * 512 + (f + 1) * 128],
                         rhs=sxb[i][:, g * TT:(g + 1) * TT],
                         start=(g == 0), stop=(g == 7),
                         inc=(("ts", 1),) if g == 7 else ())
                p.wait("vector", "ts", p.n("ts"))
                p.wait("vector", "ss", p.n("ss"))
                hslot = har[:, (e * 4 + f) * 512:(e * 4 + f + 1) * 512]
                if e < 8:
                    p.op("vector", "tensor_tensor", sh1[:], ssil[:], pb[1][:],
                         op=ALU.mult, inc=(("vs", 1),))
                    p.wait("vector", "vs", p.n("vs"))
                    p.op("vector", "tensor_tensor", hslot, sh1[:], pb[2][:],
                         op=ALU.mult, inc=(("vs", 1),))
                else:
                    p.op("vector", "tensor_tensor", hslot, ssil[:], pb[1][:],
                         op=ALU.mult, inc=(("vs", 1),))
            if e < 8:
                p.op("tensor", "nop", wait=(("vs", p.n("vs")),),
                     inc=(("s5g", 1),))
        def load_wds(e, o, buf, guard):
            if e < 8:
                wsrc = k.dram("wdT")[e][:, o * 128:(o + 1) * 128]
                p.op("sync", "dma_start",
                     out=swd[buf][:].rearrange("q (g o) -> q g o", g=4),
                     in_=wsrc.rearrange("(g q) o -> q g o", q=128),
                     wait=guard, inc=(("ld", 16),))

        p.wait("tensor", "vs", p.n("vs"))
        for o in range(8):
            load_wds(0, o, 0, (("s5d", p.n("s5d")),))
            load_wds(1, o, 1, ())
            bank = pb[o % 2]
            if o >= 2:
                p.wait("tensor", "vs", p.n("vs") - 1)
            for e in range(9):
                we = e % 2
                ld_need = p.n("ld")
                if e + 2 < 9:
                    load_wds(e + 2, o, (e + 2) % 2, (("s5d", p.n("s5d") + 1),))
                if e < 8:
                    p.wait("tensor", "ld", ld_need)
                for f in range(4):
                    hs = har[:, (e * 4 + f) * 512:(e * 4 + f + 1) * 512]
                    if e < 8:
                        lh = swd[we][:, f * 128:(f + 1) * 128]
                    else:
                        lh = ssd[:, f * 1024 + o * 128:f * 1024 + (o + 1) * 128]
                    p.op("tensor", "matmul", bank[:], lhsT=lh, rhs=hs,
                         start=(e == 0 and f == 0), stop=(e == 8 and f == 3),
                         inc=(("ts", 1),) if (e == 8 and f == 3) else ())
                if e < 8:
                    p.op("tensor", "nop", inc=(("s5d", 1),))
            p.wait("vector", "ts", p.n("ts"))
            p.wait("vector", "ld", p.n("ld"))
            p.op("vector", "tensor_tensor", ysb[:, o * TT:(o + 1) * TT],
                 bank[:], x1f[:, o * TT:(o + 1) * TT], op=ALU.add,
                 inc=(("vs", 1),))
        p.wait("scalar", "vs", p.n("vs"))
        p.wait("tensor", "vs", p.n("vs"))
        p.wait("vector", "st", p.n("st"))
        if j == B_PER_CORE * NTILE - 1 and DBG:
            p.wait("sync", "vs", p.n("vs"))
            p.op("sync", "dma_start", out=k.dram("dbg_har"), in_=har[:],
                 inc=(("st", 16),))
            p.op("sync", "dma_start", out=k.dram("dbg_y"), in_=ysb[:],
                 inc=(("st", 16),))
        emit_ln(k, ysb, ssq, rows, psm, pss, pb[6][:], pb[7][:], tmp, xout)
        p.wait("sync", "vs", p.n("vs"))
        p.op("sync", "dma_start",
             out=k.dram("xT2")[b, :, t0:t0 + TT].rearrange("(g q) t -> q g t", q=128),
             in_=xout[:].rearrange("q (g t) -> q g t", g=8),
             inc=(("st", 16),))
        p.op("sync", "nop", wait=(("st", p.n("st")),), inc=(("s5i", 1),))
    p.wait("sync", "s5i", p.n("s5i"))
    p.barrier()
    st.close()



def s6a_inproj(k: K):
    nc, p = k.nc, k.p
    st = k.stage_stack()
    sinw = st.enter_context(nc.sbuf_tensor("s6_inw", [128, 8 * 2048], BF16))
    x2f = [st.enter_context(nc.sbuf_tensor(f"s6_x2{i}", [128, 4096], F32))
           for i in range(2)]
    sqa = st.enter_context(nc.sbuf_tensor("s6_sqa", [128, 4096], F32))
    xnb = st.enter_context(nc.sbuf_tensor("s6_xnb", [128, 4096], BF16))
    uar = st.enter_context(nc.sbuf_tensor("s6_uar", [128, 4096], F32))
    gar = st.enter_context(nc.sbuf_tensor("s6_gar", [128, 4096], F32))
    rows = st.enter_context(nc.sbuf_tensor("s6_rows", [1, 512], F32))
    tmp = st.enter_context(nc.sbuf_tensor("s6_tmp", [128, 512], F32))
    pbk = [st.enter_context(nc.psum_tensor(f"s6_pb{i}", [128, 512], F32))
           for i in range(2)]
    psq = st.enter_context(nc.psum_tensor("s6_psq", [1, 512], F32))
    psr = st.enter_context(nc.psum_tensor("s6_psr", [128, 512], F32))

    p.op("sync", "dma_start", out=sinw[:].rearrange("q (g o) -> q g o", g=8),
         in_=k.dram("inwT").rearrange("(g q) o -> q g o", q=128),
         inc=(("ld", 16),))
    p.wait("tensor", "ld", p.n("ld"))

    for j in range(B_PER_CORE * NTILE):
        b, ti = divmod(j, NTILE)
        t0 = ti * TT
        i = j % 2
        w = (("s6ai", j - 1),) if j >= 2 else ()
        p.op("sync", "dma_start", out=x2f[i][:].rearrange("q (g t) -> q g t", g=8),
             in_=k.dram("xT2")[b, :, t0:t0 + TT].rearrange("(g q) t -> q g t", q=128),
             wait=w, inc=(("ld", 16),))
        ld_x = p.n("ld")
        p.wait("scalar", "ld", ld_x)
        p.wait("scalar", "ts", p.n("ts"))
        p.wait("scalar", "st", p.n("st"))
        for o in range(8):
            p.op("scalar", "activation", sqa[:, o * TT:(o + 1) * TT],
                 x2f[i][:, o * TT:(o + 1) * TT], AF.Square,
                 inc=(("ss", 1),) if o == 7 else ())
        p.wait("tensor", "ss", p.n("ss"))
        p.wait("tensor", "vs", p.n("vs"))
        for o in range(8):
            p.op("tensor", "matmul", psq[:], lhsT=k.g["onesc_f"][:],
                 rhs=sqa[:, o * TT:(o + 1) * TT], start=(o == 0), stop=(o == 7),
                 inc=(("ts", 1),) if o == 7 else ())
        p.wait("scalar", "ts", p.n("ts"))
        p.op("scalar", "activation", rows[0:1, :], psq[:], AF.Ln,
             scale=k.g["consts"]["inv1024"][0:1, :],
             bias=k.g["consts"]["eps6"][0:1, :], inc=(("ss", 1),))
        p.wait("scalar", "ss", p.n("ss"))
        p.op("scalar", "activation", rows[0:1, :], rows[0:1, :], AF.Exp,
             scale=k.g["consts"]["neg_half"][0:1, :], inc=(("ss", 1),))
        p.wait("tensor", "ss", p.n("ss"))
        p.wait("tensor", "vs", p.n("vs"))
        p.op("tensor", "matmul", psr[:], lhsT=k.g["onesr_f"][:],
             rhs=rows[0:1, :], start=True, stop=True, inc=(("ts", 1),))
        p.wait("vector", "ts", p.n("ts"))
        p.wait("vector", "ld", ld_x)
        for o in range(8):
            p.op("vector", "tensor_tensor", tmp[:], x2f[i][:, o * TT:(o + 1) * TT],
                 psr[:], op=ALU.mult)
            p.op("vector", "tensor_scalar", xnb[:, o * TT:(o + 1) * TT],
                 tmp[:], vcol(k, "norm_w", o), None, op0=ALU.mult,
                 inc=(("vs", 1),) if o == 7 else ())
        p.wait("tensor", "vs", p.n("vs"))
        for o in range(16):
            bank = pbk[o % 2]
            if o >= 2:
                p.wait("tensor", "ss", p.n("ss") - 1)
            for g in range(8):
                p.op("tensor", "matmul", bank[:],
                     lhsT=sinw[:, g * 2048 + o * 128:g * 2048 + (o + 1) * 128],
                     rhs=xnb[:, g * TT:(g + 1) * TT],
                     start=(g == 0), stop=(g == 7),
                     inc=(("ts", 1),) if g == 7 else ())
            p.wait("scalar", "ts", p.n("ts"))
            if o < 8:
                p.op("scalar", "activation", uar[:, o * TT:(o + 1) * TT],
                     bank[:], AF.Identity, bias=vcol(k, "in_b", o),
                     inc=(("ss", 1),))
            else:
                p.op("scalar", "activation", gar[:, (o - 8) * TT:(o - 7) * TT],
                     bank[:], AF.Silu, bias=vcol(k, "in_b", o),
                     inc=(("ss", 1),))
        p.wait("sync", "ss", p.n("ss"))
        p.op("sync", "dma_start",
             out=k.dram("u_pre")[b, :, t0:t0 + TT].rearrange("(g q) t -> q g t", q=128),
             in_=uar[:].rearrange("q (g t) -> q g t", g=8), inc=(("st", 16),))
        p.op("sync", "dma_start",
             out=k.dram("sgate")[b, :, t0:t0 + TT].rearrange("(g q) t -> q g t", q=128),
             in_=gar[:].rearrange("q (g t) -> q g t", g=8), inc=(("st", 16),))
        p.op("sync", "nop", wait=(("st", p.n("st")),), inc=(("s6ai", 1),))
    p.wait("sync", "s6ai", p.n("s6ai"))
    p.barrier()
    st.close()


def s6b_conv(k: K):
    nc, p = k.nc, k.p
    st = k.stage_stack()
    buf = [st.enter_context(nc.sbuf_tensor(f"s6b_u{i}", [128, 4100], F32))
           for i in range(2)]
    acc = st.enter_context(nc.sbuf_tensor("s6b_acc", [128, 4096], F32))
    ucf_s = st.enter_context(nc.sbuf_tensor("s6b_ucf", [128, 4096], F32))
    ucb_s = st.enter_context(nc.sbuf_tensor("s6b_ucb", [128, 4096], BF16))
    for j in range(B_PER_CORE * NC_CHUNK):
        b, c = divmod(j, NC_CHUNK)
        i = j % 2
        w = (("s6bi", j - 1),) if j >= 2 else ()
        p.op("vector", "memset", buf[i][:, 0:3], 0.0,
             wait=w + (("st", p.n("st")),) if j >= 2 else w, inc=(("vs", 1),))
        p.op("sync", "dma_start", out=buf[i][:, 3:4099],
             in_=k.dram("u_pre")[b, c * 128:(c + 1) * 128, :],
             wait=(("vs", p.n("vs")),), inc=(("ld", 16),))
        p.wait("vector", "ld", p.n("ld"))
        if j >= 1:
            p.wait("vector", "ss", p.n("ss"))
        p.op("vector", "tensor_scalar", acc[:], buf[i][:, 0:4096],
             vcol(k, "conv_w", 0 * 8 + c), vcol(k, "conv_b", c),
             op0=ALU.mult, op1=ALU.add)
        for kk in range(1, 4):
            p.op("vector", "scalar_tensor_tensor", acc[:], buf[i][:, kk:kk + 4096],
                 vcol(k, "conv_w", kk * 8 + c), acc[:], op0=ALU.mult, op1=ALU.add,
                 inc=(("vs", 1),) if kk == 3 else ())
        p.wait("scalar", "vs", p.n("vs"))
        p.wait("scalar", "st", p.n("st"))
        p.op("scalar", "activation", ucf_s[:], acc[:], AF.Silu, inc=(("ss", 1),))
        p.wait("vector", "ss", p.n("ss"))
        p.op("vector", "tensor_copy", out=ucb_s[:], in_=ucf_s[:], inc=(("vs", 1),))
        p.wait("sync", "vs", p.n("vs"))
        p.op("sync", "dma_start", out=k.dram("ucf")[b, c * 128:(c + 1) * 128, :],
             in_=ucf_s[:], inc=(("st", 16),))
        p.op("sync", "dma_start", out=k.dram("ucb")[b, c * 128:(c + 1) * 128, :],
             in_=ucb_s[:], inc=(("st", 16),))
        p.op("sync", "nop", wait=(("st", p.n("st")),), inc=(("s6bi", 1),))
    p.wait("sync", "s6bi", p.n("s6bi"))
    p.barrier()
    st.close()


def s6c_params(k: K):
    nc, p = k.nc, k.p
    st = k.stage_stack()
    spw = st.enter_context(nc.sbuf_tensor("s6c_pw", [128, 8 * 3072], BF16))
    ucb = [st.enter_context(nc.sbuf_tensor(f"s6c_ub{i}", [128, 4096], BF16))
           for i in range(2)]
    ucf = [st.enter_context(nc.sbuf_tensor(f"s6c_uf{i}", [128, 4096], F32))
           for i in range(2)]
    aa = st.enter_context(nc.sbuf_tensor("s6c_aa", [128, 4096], F32))
    bu = st.enter_context(nc.sbuf_tensor("s6c_bu", [128, 4096], F32))
    ca = st.enter_context(nc.sbuf_tensor("s6c_ca", [128, 4096], F32))
    bt = st.enter_context(nc.sbuf_tensor("s6c_bt", [128, 512], F32))
    pbk = [st.enter_context(nc.psum_tensor(f"s6c_pb{i}", [128, 512], F32))
           for i in range(2)]

    p.op("sync", "dma_start", out=spw[:].rearrange("q (g o) -> q g o", g=8),
         in_=k.dram("pwT").rearrange("(g q) o -> q g o", q=128),
         inc=(("ld", 16),))
    p.wait("tensor", "ld", p.n("ld"))

    for j in range(B_PER_CORE * NTILE):
        b, ti = divmod(j, NTILE)
        t0 = ti * TT
        i = j % 2
        w = (("s6ci", j - 1),) if j >= 2 else ()
        p.op("sync", "dma_start", out=ucb[i][:].rearrange("q (g t) -> q g t", g=8),
             in_=k.dram("ucb")[b, :, t0:t0 + TT].rearrange("(g q) t -> q g t", q=128),
             wait=w, inc=(("ld", 16),))
        p.op("sync", "dma_start", out=ucf[i][:].rearrange("q (g t) -> q g t", g=8),
             in_=k.dram("ucf")[b, :, t0:t0 + TT].rearrange("(g q) t -> q g t", q=128),
             inc=(("ld", 16),))
        ld_x = p.n("ld")
        p.wait("tensor", "ld", ld_x)
        st_before = p.n("st")
        if j >= 1:
            p.wait("scalar", "st", st_before)
            p.wait("vector", "st", st_before)
        for o in range(24):
            bank = pbk[o % 2]
            if o >= 2:
                p.wait("tensor", "ss", p.n("ss") - 1)
            for g in range(8):
                p.op("tensor", "matmul", bank[:],
                     lhsT=spw[:, g * 3072 + o * 128:g * 3072 + (o + 1) * 128],
                     rhs=ucb[i][:, g * TT:(g + 1) * TT],
                     start=(g == 0), stop=(g == 7),
                     inc=(("ts", 1),) if g == 7 else ())
            p.wait("scalar", "ts", p.n("ts"))
            if o < 8:
                p.op("scalar", "activation", aa[:, o * TT:(o + 1) * TT],
                     bank[:], AF.Sigmoid, bias=vcol(k, "param_b", o),
                     inc=(("ss", 1),))
            elif o < 16:
                oo = o - 8
                p.op("scalar", "activation", bt[:], bank[:], AF.Tanh,
                     bias=vcol(k, "param_b", o), inc=(("ss", 1),))
                p.wait("vector", "ss", p.n("ss"))
                p.wait("vector", "ld", ld_x)
                p.op("vector", "tensor_tensor", bu[:, oo * TT:(oo + 1) * TT],
                     bt[:], ucf[i][:, oo * TT:(oo + 1) * TT], op=ALU.mult,
                     inc=(("ss", 1),))
            else:
                p.op("scalar", "activation", ca[:, (o - 16) * TT:(o - 15) * TT],
                     bank[:], AF.Sigmoid, bias=vcol(k, "param_b", o),
                     inc=(("ss", 1),))
        p.wait("sync", "ss", p.n("ss"))
        for nm, arr in (("aT", aa), ("buT", bu), ("cT", ca)):
            p.op("sync", "dma_start",
                 out=k.dram(nm)[b, :, t0:t0 + TT].rearrange("(g q) t -> q g t", q=128),
                 in_=arr[:].rearrange("q (g t) -> q g t", g=8), inc=(("st", 16),))
        p.op("sync", "nop", wait=(("st", p.n("st")),), inc=(("s6ci", 1),))
    p.wait("sync", "s6ci", p.n("s6ci"))
    p.barrier()
    st.close()


def s6d_scan(k: K):
    nc, p = k.nc, k.p
    st = k.stage_stack()
    abuf = st.enter_context(nc.sbuf_tensor("s6d_a", [128, 4096], F32))
    bbuf = st.enter_context(nc.sbuf_tensor("s6d_b", [128, 4096], F32))
    cbuf = st.enter_context(nc.sbuf_tensor("s6d_c", [128, 4096], F32))
    ubuf = st.enter_context(nc.sbuf_tensor("s6d_u", [128, 4096], F32))
    gbuf = st.enter_context(nc.sbuf_tensor("s6d_g", [128, 4096], F32))
    sst = st.enter_context(nc.sbuf_tensor("s6d_s", [128, 4096], F32))
    y2s = st.enter_context(nc.sbuf_tensor("s6d_y", [128, 4096], BF16))
    for j in range(B_PER_CORE * NC_CHUNK):
        b, c = divmod(j, NC_CHUNK)
        w = (("s6di", j),) if j >= 1 else ()
        for nm, arr in (("aT", abuf), ("buT", bbuf), ("cT", cbuf),
                        ("ucf", ubuf), ("sgate", gbuf)):
            p.op("sync", "dma_start", out=arr[:],
                 in_=k.dram(nm)[b, c * 128:(c + 1) * 128, :],
                 wait=w if nm == "aT" else (), inc=(("ld", 16),))
        p.wait("vector", "ld", p.n("ld"))
        p.op("vector", "tensor_tensor_scan", sst[:], abuf[:], bbuf[:], 0.0,
             op0=ALU.mult, op1=ALU.add, inc=(("vs", 1),))
        p.wait("vector", "vs", p.n("vs"))
        p.op("vector", "tensor_tensor", abuf[:], cbuf[:], sst[:], op=ALU.mult,
             inc=(("vs", 1),))
        p.wait("vector", "vs", p.n("vs"))
        p.op("vector", "scalar_tensor_tensor", abuf[:], ubuf[:],
             vcol(k, "skip", c), abuf[:], op0=ALU.mult, op1=ALU.add,
             inc=(("vs", 1),))
        p.wait("vector", "vs", p.n("vs"))
        p.op("vector", "tensor_tensor", y2s[:], abuf[:], gbuf[:], op=ALU.mult,
             inc=(("vs", 1),))
        p.wait("sync", "vs", p.n("vs"))
        p.op("sync", "dma_start", out=k.dram("y2b")[b, c * 128:(c + 1) * 128, :],
             in_=y2s[:], inc=(("st", 16),))
        p.op("sync", "nop", wait=(("st", p.n("st")),), inc=(("s6di", 1),))
    p.wait("sync", "s6di", p.n("s6di"))
    p.barrier()
    st.close()


def s6e_outproj(k: K):
    nc, p = k.nc, k.p
    st = k.stage_stack()
    sow = st.enter_context(nc.sbuf_tensor("s6e_ow", [128, 8 * 1024], BF16))
    y2b = [st.enter_context(nc.sbuf_tensor(f"s6e_y2{i}", [128, 4096], BF16))
           for i in range(2)]
    x2f = [st.enter_context(nc.sbuf_tensor(f"s6e_x2{i}", [128, 4096], F32))
           for i in range(2)]
    ysb = st.enter_context(nc.sbuf_tensor("s6e_y", [128, 4096], F32))
    xout = st.enter_context(nc.sbuf_tensor("s6e_xo", [128, 4096], F32))
    ssq = xout
    rows = st.enter_context(nc.sbuf_tensor("s6e_rows", [1, 2048], F32))
    tmp = st.enter_context(nc.sbuf_tensor("s6e_tmp", [128, 512], F32))
    pb = [st.enter_context(nc.psum_tensor(f"s6e_pb{i}", [128, 512], F32))
          for i in range(6)]
    psm = pb[2][0:1, :]
    pss = pb[3][0:1, :]

    p.op("sync", "dma_start", out=sow[:].rearrange("q (g o) -> q g o", g=8),
         in_=k.dram("owT").rearrange("(g q) o -> q g o", q=128),
         inc=(("ld", 16),))
    p.wait("tensor", "ld", p.n("ld"))

    for j in range(B_PER_CORE * NTILE):
        b, ti = divmod(j, NTILE)
        t0 = ti * TT
        i = j % 2
        w = (("s6ei", j - 1),) if j >= 2 else ()
        p.op("sync", "dma_start", out=y2b[i][:].rearrange("q (g t) -> q g t", g=8),
             in_=k.dram("y2b")[b, :, t0:t0 + TT].rearrange("(g q) t -> q g t", q=128),
             wait=w, inc=(("ld", 16),))
        p.op("sync", "dma_start", out=x2f[i][:].rearrange("q (g t) -> q g t", g=8),
             in_=k.dram("xT2")[b, :, t0:t0 + TT].rearrange("(g q) t -> q g t", q=128),
             inc=(("ld", 16),))
        p.wait("tensor", "ld", p.n("ld"))
        p.wait("vector", "ld", p.n("ld"))
        p.wait("tensor", "vs", p.n("vs"))
        for o in range(8):
            bank = pb[o % 2]
            if o >= 2:
                p.wait("tensor", "vs", p.n("vs") - 1)
            for g in range(8):
                p.op("tensor", "matmul", bank[:],
                     lhsT=sow[:, g * 1024 + o * 128:g * 1024 + (o + 1) * 128],
                     rhs=y2b[i][:, g * TT:(g + 1) * TT],
                     start=(g == 0), stop=(g == 7),
                     inc=(("ts", 1),) if g == 7 else ())
            p.wait("vector", "ts", p.n("ts"))
            p.op("vector", "scalar_tensor_tensor", tmp[:],
                 x2f[i][:, o * TT:(o + 1) * TT], 2.0, bank[:],
                 op0=ALU.mult, op1=ALU.add)
            p.op("vector", "tensor_scalar", ysb[:, o * TT:(o + 1) * TT],
                 tmp[:], vcol(k, "out_b", o), None, op0=ALU.add,
                 inc=(("vs", 1),))
        p.wait("scalar", "vs", p.n("vs"))
        p.wait("tensor", "vs", p.n("vs"))
        p.wait("vector", "st", p.n("st"))
        emit_ln(k, ysb, ssq, rows, psm, pss, pb[4][:], pb[5][:], tmp, xout)
        p.wait("sync", "vs", p.n("vs"))
        p.op("sync", "dma_start",
             out=k.dram("xT3")[b, :, t0:t0 + TT].rearrange("(g q) t -> q g t", q=128),
             in_=xout[:].rearrange("q (g t) -> q g t", g=8),
             inc=(("st", 16),))
        p.op("sync", "nop", wait=(("st", p.n("st")),), inc=(("s6ei", 1),))
    p.wait("sync", "s6ei", p.n("s6ei"))
    p.barrier()
    st.close()



def s7_peer(k: K):
    nc, p = k.nc, k.p
    st = k.stage_stack()
    sWq = st.enter_context(nc.sbuf_tensor("s7_wq", [128, 8 * 512], BF16))
    skeys = st.enter_context(nc.sbuf_tensor("s7_keys", [128, 512], BF16))
    xb = [st.enter_context(nc.sbuf_tensor(f"s7_x{i}", [128, 4096], BF16))
          for i in range(2)]
    x3f = st.enter_context(nc.sbuf_tensor("s7_x3", [128, 4096], F32))
    qTb_s = st.enter_context(nc.sbuf_tensor("s7_q", [128, 2048], BF16))
    wsm = [st.enter_context(nc.sbuf_tensor(f"s7_ws{i}", [128, 1024], BF16))
           for i in range(2)]
    wbig = [st.enter_context(nc.sbuf_tensor(f"s7_wb{i}", [128, 4096], BF16))
            for i in range(4)]
    dal2 = [st.enter_context(nc.sbuf_tensor(f"s7_da{i}", [128, 512], F32))
            for i in range(2)]
    har32 = st.enter_context(nc.sbuf_tensor("s7_har", [128, 32 * 512], BF16))
    sst = st.enter_context(nc.sbuf_tensor("s7_st", [128, 32 * 512], BF16))
    sim_s = st.enter_context(nc.sbuf_tensor("s7_sim", [128, 512], F32))
    q64 = st.enter_context(nc.sbuf_tensor("s7_q64", [64, 128], BF16))
    t8a = st.enter_context(nc.sbuf_tensor("s7_t8a", [128, 8], F32))
    t8b = st.enter_context(nc.sbuf_tensor("s7_t8b", [128, 8], F32))
    ix8 = st.enter_context(nc.sbuf_tensor("s7_ix8", [128, 8], F32))
    iy8 = st.enter_context(nc.sbuf_tensor("s7_iy8", [128, 8], F32))
    s8 = st.enter_context(nc.sbuf_tensor("s7_s8", [128, 8], F32))
    alls = st.enter_context(nc.sbuf_tensor("s7_alls", [128, 64], F32))
    alli = st.enter_context(nc.sbuf_tensor("s7_alli", [128, 64], F32))
    junk = st.enter_context(nc.sbuf_tensor("s7_junk", [128, 64], F32))
    p8st = st.enter_context(nc.sbuf_tensor("s7_p8", [128, 128], F32))
    fidxst = st.enter_context(nc.sbuf_tensor("s7_fidx", [128, 128], F32))
    offf = st.enter_context(nc.sbuf_tensor("s7_offf", [128, 128], F32))
    offi = st.enter_context(nc.sbuf_tensor("s7_offi", [128, 128], I32))
    dsel = st.enter_context(nc.sbuf_tensor("s7_dsel", [128, 128], F32))
    wsil = st.enter_context(nc.sbuf_tensor("s7_wsil", [128, 128], F32))
    wsb = st.enter_context(nc.sbuf_tensor("s7_wsb", [128, 128], BF16))
    tgc = st.enter_context(nc.sbuf_tensor("s7_tg", [128, 4], F32))
    tgi = st.enter_context(nc.sbuf_tensor("s7_tgi", [128, 4], I32))
    ysb = st.enter_context(nc.sbuf_tensor("s7_y", [128, 4096], F32))
    xout = st.enter_context(nc.sbuf_tensor("s7_xo", [128, 4096], F32))
    ssq = xout
    rows = st.enter_context(nc.sbuf_tensor("s7_rows", [1, 2048], F32))
    tmp = st.enter_context(nc.sbuf_tensor("s7_tmp", [128, 512], F32))
    sout = st.enter_context(nc.sbuf_tensor("s7_out", [128, 1024], F32))
    pb = [st.enter_context(nc.psum_tensor(f"s7_pb{i}", [128, 512], F32))
          for i in range(8)]
    psm = pb[3][0:1, :]
    pss = pb[4][0:1, :]

    dallf = k.dram("dall").rearrange("a b -> (a b)").unsqueeze(1)
    STf = k.dram("ST").rearrange("a b -> (a b)").unsqueeze(1)

    p.op("sync", "dma_start", out=sWq[:].rearrange("q (g o) -> q g o", g=8),
         in_=k.dram("WqT").rearrange("(g q) o -> q g o", q=128),
         inc=(("ld", 16),))
    p.op("sync", "dma_start",
         out=skeys[0:64, :].rearrange("n (hp kk) -> n hp kk", kk=64),
         in_=k.dram("keysT").rearrange("h pp n kk -> n (h pp) kk"),
         inc=(("ld", 16),))
    p.op("sync", "dma_start",
         out=skeys[64:128, :].rearrange("n (hp kk) -> n hp kk", kk=64),
         in_=k.dram("keysT").rearrange("h pp n kk -> n (h pp) kk"),
         inc=(("ld", 16),))
    p.wait("tensor", "ld", p.n("ld"))
    p.op("vector", "memset", har32[:, 0:8192], 0.0, inc=(("vs", 1),))
    p.wait("sync", "vs", p.n("vs"))
    for blk in range(32):
        p.op("sync", "dma_start",
             out=k.dram("ST")[blk * 128:(blk + 1) * 128, :],
             in_=har32[:, 0:8192].bitcast(BF16) if False else har32[:, 0:8192],
             inc=(("st", 16),))
    p.op("vector", "memset", ysb[:], 0.0, inc=(("vs", 1),))
    p.wait("sync", "vs", p.n("vs"))
    for blk in range(32):
        for hf in range(2):
            p.op("sync", "dma_start",
                 out=k.dram("dall")[blk * 128:(blk + 1) * 128,
                                    hf * 4096:(hf + 1) * 4096],
                 in_=ysb[:], inc=(("st", 16),))
    st_zero = p.n("st")

    gq_hist = []

    for j in range(min(S7_NT, B_PER_CORE * NTILE)):
        b, ti = divmod(j, NTILE)
        t0 = ti * TT
        bt0 = b * T + t0
        i = j % 2
        w = (("s7i", j - 1),) if j >= 2 else ()
        p.op("gpsimd", "dma_start", out=xb[i][:].rearrange("q (g t) -> q g t", g=8),
             in_=k.dram("xT3")[b, :, t0:t0 + TT].rearrange("(g q) t -> q g t", q=128),
             wait=w, inc=(("gd", 16),))
        wx = (("s7i", j),) if j >= 1 else ()
        p.op("sync", "dma_start", out=x3f[:].rearrange("q (g t) -> q g t", g=8),
             in_=k.dram("xT3")[b, :, t0:t0 + TT].rearrange("(g q) t -> q g t", q=128),
             wait=wx, inc=(("ld", 16),))
        p.wait("tensor", "gd", p.n("gd"))
        p.wait("tensor", "vs", p.n("vs"))
        p.wait("tensor", "ss", p.n("ss"))
        for o4 in range(4):
            bank = pb[o4 % 2]
            if o4 >= 2:
                p.wait("tensor", "ss", p.n("ss") - 1)
            for g in range(8):
                p.op("tensor", "matmul", bank[:],
                     lhsT=sWq[:, g * 512 + o4 * 128:g * 512 + (o4 + 1) * 128],
                     rhs=xb[i][:, g * TT:(g + 1) * TT],
                     start=(g == 0), stop=(g == 7),
                     inc=(("ts", 1),) if g == 7 else ())
            p.wait("scalar", "ts", p.n("ts"))
            p.op("scalar", "copy", out=qTb_s[:, o4 * TT:(o4 + 1) * TT],
                 in_=bank[:], inc=(("ss", 1),))
        p.wait("tensor", "ss", p.n("ss"))
        for o in range(32 if S7_PH >= 2 else 0):
            bw = o % 2
            guard = (("s7w", p.n("s7w")),) if o >= 2 else (("s7w", p.n("s7w")),)
            p.op("sync", "dma_start",
                 out=wsm[bw][:].rearrange("q (g c) -> q g c", g=8),
                 in_=k.dram("downT")[:, o * 128:(o + 1) * 128].rearrange(
                     "(g q) c -> q g c", q=128),
                 wait=guard if o >= 2 else (), inc=(("ld", 16),))
            p.wait("tensor", "ld", p.n("ld"))
            bank = pb[bw]
            if o >= 2:
                p.wait("tensor", "vs", p.n("vs") - 1)
            for g in range(8):
                p.op("tensor", "matmul", bank[:],
                     lhsT=wsm[bw][:, g * 128:(g + 1) * 128],
                     rhs=xb[i][:, g * TT:(g + 1) * TT],
                     start=(g == 0), stop=(g == 7),
                     inc=(("ts", 1),) if g == 7 else ())
            p.op("tensor", "nop", inc=(("s7w", 1),))
            p.wait("vector", "ts", p.n("ts"))
            if o >= 2:
                p.wait("vector", "st", p.n("st") - 16)
            p.op("vector", "tensor_copy", out=dal2[bw][:], in_=bank[:],
                 inc=(("vs", 1),))
            p.wait("sync", "vs", p.n("vs"))
            p.op("sync", "dma_start",
                 out=k.dram("dall")[o * 128:(o + 1) * 128, bt0:bt0 + TT],
                 in_=dal2[bw][:], inc=(("st", 16),))
        st_dall = p.n("st")
        p.wait("tensor", "vs", p.n("vs"))
        for o in range(32 if S7_PH >= 3 else 0):
            bw = o % 2
            p.op("sync", "dma_start",
                 out=wsm[bw][:].rearrange("q (g c) -> q g c", g=8),
                 in_=k.dram("WupT")[:, o * 128:(o + 1) * 128].rearrange(
                     "(g q) c -> q g c", q=128),
                 wait=(("s7w", p.n("s7w")),) if True else (),
                 inc=(("ld", 16),))
            p.wait("tensor", "ld", p.n("ld"))
            bank = pb[bw]
            if o >= 2:
                p.wait("tensor", "ss", p.n("ss") - 1)
            for g in range(8):
                p.op("tensor", "matmul", bank[:],
                     lhsT=wsm[bw][:, g * 128:(g + 1) * 128],
                     rhs=xb[i][:, g * TT:(g + 1) * TT],
                     start=(g == 0), stop=(g == 7),
                     inc=(("ts", 1),) if g == 7 else ())
            p.op("tensor", "nop", inc=(("s7w", 1),))
            p.wait("scalar", "ts", p.n("ts"))
            p.op("scalar", "activation", har32[:, o * TT:(o + 1) * TT],
                 bank[:], AF.Silu, inc=(("ss", 1),))
        p.wait("tensor", "ss", p.n("ss"))
        if gq_hist:
            p.wait("vector", "gq", p.n("gq"))
        for cc in range(4 if S7_PH >= 4 else 0):
            for ph in range(8):
                pp, hh = divmod(ph, 4)
                qdim = pp * 256 + hh * 64
                o4, row0 = divmod(qdim, 128)
                kb = hh * 2 + pp
                p.wait("vector", "ts", p.n("ts"))
                p.op("vector", "tensor_copy", out=q64[:],
                     in_=qTb_s[row0:row0 + 64,
                               o4 * TT + cc * 128:o4 * TT + (cc + 1) * 128],
                     inc=(("vs", 1),))
                p.wait("tensor", "vs", p.n("vs"))
                p.op("tensor", "matmul", pb[2][:, 0:64],
                     lhsT=q64[:],
                     rhs=skeys[0:64, kb * 64:(kb + 1) * 64],
                     start=True, stop=True, inc=(("ts", 1),))
                p.wait("vector", "ts", p.n("ts"))
                p.op("vector", "tensor_copy",
                     out=sim_s[:, ph * 64:(ph + 1) * 64], in_=pb[2][:, 0:64],
                     inc=(("vs", 1),))
            p.wait("vector", "ss", p.n("ss"))
            p.wait("vector", "vs", p.n("vs"))
            if S7_SKIP_TOPK:
                continue
            for hh in range(4):
                sx_sl = sim_s[:, (0 + hh) * 64:(1 + hh) * 64]
                sy_sl = sim_s[:, (4 + hh) * 64:(5 + hh) * 64]
                p.op("vector", "max", out=t8a[:], in_=sx_sl, inc=(("vs", 1),))
                p.op("vector", "max", out=t8b[:], in_=sy_sl, inc=(("vs", 1),))
                p.wait("vector", "vs", p.n("vs"))
                for r in range(8):
                    p.op("vector", "scalar_tensor_tensor", junk[:], sx_sl,
                         t8a[:, r:r + 1], k.g["iota64f"][:], op0=ALU.is_equal,
                         op1=ALU.mult, accum_out=ix8[:, r:r + 1])
                for r in range(8):
                    p.op("vector", "scalar_tensor_tensor", junk[:], sy_sl,
                         t8b[:, r:r + 1], k.g["iota64f"][:], op0=ALU.is_equal,
                         op1=ALU.mult, accum_out=iy8[:, r:r + 1],
                         inc=(("vs", 1),) if r == 7 else ())
                p.wait("vector", "vs", p.n("vs"))
                p.op("vector", "tensor_scalar", ix8[:], ix8[:], 64.0, None,
                     op0=ALU.mult, inc=(("vs", 1),))
                p.wait("vector", "vs", p.n("vs"))
                for r in range(8):
                    p.op("vector", "tensor_scalar", alls[:, 8 * r:8 * (r + 1)],
                         t8b[:], t8a[:, r:r + 1], None, op0=ALU.add)
                for r in range(8):
                    p.op("vector", "tensor_scalar", alli[:, 8 * r:8 * (r + 1)],
                         iy8[:], ix8[:, r:r + 1], None, op0=ALU.add,
                         inc=(("vs", 1),) if r == 7 else ())
                p.wait("vector", "vs", p.n("vs"))
                p.op("vector", "max", out=s8[:], in_=alls[:], inc=(("vs", 1),))
                p.wait("vector", "vs", p.n("vs"))
                for r in range(8):
                    p.op("vector", "scalar_tensor_tensor", junk[:], alls[:],
                         s8[:, r:r + 1], alli[:], op0=ALU.is_equal, op1=ALU.mult,
                         accum_out=fidxst[:, cc * 32 + hh * 8 + r:
                                          cc * 32 + hh * 8 + r + 1],
                         inc=(("vs", 1),) if r == 7 else ())
                p.wait("vector", "vs", p.n("vs"))
                p.op("vector", "tensor_scalar", t8a[:, 0:1], s8[:, 0:1], -1.0,
                     None, op0=ALU.mult, inc=(("vs", 1),))
                p.wait("scalar", "vs", p.n("vs"))
                p.op("scalar", "activation", t8b[:], s8[:], AF.Exp,
                     bias=t8a[:, 0:1], inc=(("ss", 1),))
                p.wait("vector", "ss", p.n("ss"))
                p.op("vector", "reduce_sum", t8a[:, 1:2], t8b[:],
                     axis=mybir.AxisListType.X, inc=(("vs", 1),))
                p.wait("vector", "vs", p.n("vs"))
                p.op("vector", "reciprocal", t8a[:, 2:3], t8a[:, 1:2],
                     inc=(("vs", 1),))
                p.wait("vector", "vs", p.n("vs"))
                p.op("vector", "tensor_scalar",
                     p8st[:, cc * 32 + hh * 8:cc * 32 + hh * 8 + 8],
                     t8b[:], t8a[:, 2:3], None, op0=ALU.mult, inc=(("vs", 1),))
            p.op("vector", "tensor_scalar", tgc[:, cc:cc + 1], k.g["iotaPf"][:],
                 1.0, float(bt0 + cc * 128), op0=ALU.mult, op1=ALU.add,
                 inc=(("vs", 1),))
            p.wait("vector", "vs", p.n("vs"))
            p.op("vector", "tensor_copy", out=tgi[:, cc:cc + 1],
                 in_=tgc[:, cc:cc + 1], inc=(("vs", 1),))
            sl32 = slice(cc * 32, (cc + 1) * 32)
            p.op("vector", "tensor_scalar", offf[:, sl32],
                 fidxst[:, sl32], 0.0, 4095.0, op0=ALU.max, op1=ALU.min)
            p.op("vector", "tensor_scalar", offf[:, sl32],
                 offf[:, sl32], 8192.0, None, op0=ALU.mult, inc=(("vs", 1),))
            p.wait("vector", "vs", p.n("vs"))
            p.op("vector", "tensor_copy", out=offi[:, sl32], in_=offf[:, sl32],
                 inc=(("vs", 1),))
            p.wait("vector", "vs", p.n("vs"))
            p.op("vector", "tensor_scalar", offi[:, sl32], offi[:, sl32],
                 tgi[:, cc:cc + 1], None, op0=ALU.bitwise_or, inc=(("vs", 1),))
        vs_off = p.n("vs")
        p.wait("gpsimd", "vs", vs_off)
        p.wait("gpsimd", "st", st_dall)
        if S7_SKIP_INDIRECT:
            p.op("vector", "memset", dsel[:], 0.0, inc=(("gq", 16),))
            gq_hist.append(p.n("gq"))
        else:
            for s in range(128):
                if len(gq_hist) >= 6:
                    p.wait("gpsimd", "gq", gq_hist[-6])
                p.op("gpsimd", "indirect_dma_start",
                     out=dsel[:, s:s + 1], out_offset=None,
                     in_=dallf, in_offset=IndirectOffsetOnAxis(offi[:, s:s + 1], 0),
                     inc=(("gq", 16),))
                gq_hist.append(p.n("gq"))
        gq_gather = p.n("gq")
        p.wait("vector", "gq", gq_gather)
        p.op("vector", "tensor_tensor", wsil[:], dsel[:], p8st[:], op=ALU.mult,
             inc=(("vs", 1),))
        p.wait("scalar", "vs", p.n("vs"))
        p.op("scalar", "activation", wsb[:], wsil[:], AF.Silu, inc=(("ss", 1),))
        vs_w = p.n("ss")
        p.wait("gpsimd", "ss", vs_w)
        if j == 0:
            p.wait("gpsimd", "st", st_zero)
        if not S7_SKIP_INDIRECT:
            for s in range(128):
                if len(gq_hist) >= 6:
                    p.wait("gpsimd", "gq", gq_hist[-6])
                p.op("gpsimd", "indirect_dma_start",
                     out=STf, out_offset=IndirectOffsetOnAxis(offi[:, s:s + 1], 0),
                     in_=wsb[:, s:s + 1], in_offset=None, compute_op=ALU.add,
                     inc=(("gq", 16),))
                gq_hist.append(p.n("gq"))
        gq_scat = p.n("gq")
        if S7_PH < 5:
            p.wait("vector", "ld", p.n("ld"))
            p.wait("vector", "ts", p.n("ts"))
            p.wait("vector", "ss", p.n("ss"))
            if j == 0:
                p.wait("vector", "st", st_zero)
            for o in range(8):
                p.op("vector", "tensor_copy", out=ysb[:, o * TT:(o + 1) * TT],
                     in_=x3f[:, o * TT:(o + 1) * TT],
                     inc=(("vs", 1),) if o == 7 else ())
        p.wait("sync", "gq", gq_scat)
        if S7_PH >= 5:
            p.op("sync", "dma_start",
                 out=sst[:].rearrange("q (c t) -> q c t", c=32),
                 in_=k.dram("ST")[:, bt0:bt0 + TT].rearrange(
                     "(c q) t -> q c t", q=128),
                 wait=(("s7d", p.n("s7d")),), inc=(("ld", 16),))
        p.wait("tensor", "ss", p.n("ss"))
        for o in range(8 if S7_PH >= 5 else 0):
            bw = o % 2
            guard = (("s7d", p.n("s7d")),) if o >= 2 else ()
            p.op("sync", "dma_start",
                 out=wbig[bw][:].rearrange("q (c z) -> q c z", c=32),
                 in_=k.dram("WdownT")[:, o * 128:(o + 1) * 128].rearrange(
                     "(c q) z -> q c z", q=128),
                 wait=guard, inc=(("ld", 16),))
            p.op("sync", "dma_start",
                 out=wbig[2 + bw][:].rearrange("q (c z) -> q c z", c=32),
                 in_=k.dram("upT")[:, o * 128:(o + 1) * 128].rearrange(
                     "(c q) z -> q c z", q=128),
                 inc=(("ld", 16),))
            p.wait("tensor", "ld", p.n("ld"))
            bank = pb[o % 2]
            if o >= 2:
                p.wait("tensor", "vs", p.n("vs") - 1)
            for c in range(32):
                p.op("tensor", "matmul", bank[:],
                     lhsT=wbig[bw][:, c * 128:(c + 1) * 128],
                     rhs=har32[:, c * TT:(c + 1) * TT],
                     start=(c == 0), stop=(S7_SKIP_UP and c == 31),
                     inc=(("ts", 1),) if (S7_SKIP_UP and c == 31) else ())
            if not S7_SKIP_UP:
                for c in range(32):
                    p.op("tensor", "matmul", bank[:],
                         lhsT=wbig[2 + bw][:, c * 128:(c + 1) * 128],
                         rhs=sst[:, c * TT:(c + 1) * TT],
                         start=False, stop=(c == 31),
                         inc=(("ts", 1),) if c == 31 else ())
            p.op("tensor", "nop", inc=(("s7d", 1),))
            p.wait("vector", "ts", p.n("ts"))
            if j == 0 and o == 0:
                p.wait("vector", "st", st_zero)
            p.op("vector", "tensor_tensor", ysb[:, o * TT:(o + 1) * TT],
                 bank[:], x3f[:, o * TT:(o + 1) * TT], op=ALU.add,
                 inc=(("vs", 1),))
        p.wait("scalar", "vs", p.n("vs"))
        p.wait("tensor", "vs", p.n("vs"))
        p.wait("vector", "st", p.n("st"))
        emit_ln(k, ysb, ssq, rows, psm, pss, pb[5][:], pb[6][:], tmp, xout)
        p.wait("tensor", "vs", p.n("vs"))
        for tj in range(4):
            for half in range(2):
                pbank = pb[2] if half == 0 else pb[7]
                p.wait("tensor", "ss", p.n("ss"))
                for oo in range(4):
                    o = half * 4 + oo
                    p.op("tensor", "transpose",
                         pbank[:, oo * 128:(oo + 1) * 128],
                         in_=xout[:, o * TT + tj * 128:o * TT + (tj + 1) * 128],
                         identity=k.g["ident_f"][:],
                         inc=(("ts", 1),) if oo == 3 else ())
                p.wait("scalar", "ts", p.n("ts"))
                p.op("scalar", "copy", out=sout[:, half * 512:(half + 1) * 512],
                     in_=pbank[:], inc=(("ss", 1),))
            p.wait("sync", "ss", p.n("ss"))
            p.op("sync", "dma_start",
                 out=k.dram("out")[b, t0 + tj * 128:t0 + (tj + 1) * 128, :],
                 in_=sout[:], inc=(("st", 16),))
            if tj < 3:
                p.wait("scalar", "st", p.n("st"))
        p.op("sync", "nop", wait=(("st", p.n("st")),), inc=(("s7i", 1),))
    p.wait("sync", "s7i", p.n("s7i"))
    p.barrier()
    st.close()



def shard_inputs(inputs):
    x = np.ascontiguousarray(inputs["L0_attn_x"])
    maps = []
    for c in range(8):
        m = {"x_in": x[2 * c:2 * c + 2]}
        for nm in WEIGHT_NAMES:
            if nm == "L0_attn_x":
                continue
            m[nm] = np.ascontiguousarray(inputs[nm])
        maps.append(m)
    return maps


STAGES = {
    "s0": s0_prep,
    "s1": s1_ingest,
    "s2": s2_qkv,
    "s3": s3_attn,
    "s4": s4_wo_ln,
    "s5": s5_moe,
    "s6a": s6a_inproj,
    "s6b": s6b_conv,
    "s6c": s6c_params,
    "s6d": s6d_scan,
    "s6e": s6e_outproj,
    "s7": s7_peer,
}


def build_program(stage_names, ext_in, ext_out):
    nc = bass.Bass("TRN2", target_bir_lowering=False,
                   detect_race_conditions=os.environ.get("RACE_DETECT", "0") == "1")
    k = K(nc, ext_in=ext_in, ext_out=ext_out)
    build_globals(k)
    if "s0" not in stage_names:
        load_vecs(k)
    for snm in stage_names:
        STAGES[snm](k)
        if snm == "s0":
            load_vecs(k)
    k.close()
    k.p.finalize()
    return nc, k


ALL_STAGES = ["s0", "s1", "s2", "s3", "s4", "s5",
              "s6a", "s6b", "s6c", "s6d", "s6e", "s7"]


def kernel(**inputs):
    ext_in = (set(WEIGHT_NAMES) - {"L0_attn_x"}) | {"x_in"}
    nc, k = build_program(ALL_STAGES, ext_in, {"out"})
    maps = shard_inputs(inputs)
    res = run_bass_kernel_spmd(nc, maps, core_ids=list(range(8)))
    out = np.concatenate([np.asarray(r["out"]) for r in res.results], axis=0)
    return out.astype(np.float32, copy=False)
